# version 25
# baseline (speedup 1.0000x reference)
import contextlib
import numpy as np

import concourse.bass as bass
import concourse.mybir as mybir
from concourse.bass_utils import run_bass_kernel_spmd
from concourse.dve_spec import Spec, Src0, Src1, C0, C1, C2, C3, One, maxx, lower
from concourse.dve_op import DveOp
import concourse.dve_ops as dve_ops_mod
from concourse.library_overlay import lower_extended_insts

F32 = mybir.dt.float32
BF16 = mybir.dt.bfloat16
AO = mybir.AluOpType
AF = mybir.ActivationFunctionType
AX = mybir.AxisListType.X

D = 1024
FF = 4096
DQ = 64
NCORE = 8

PWL_Y_BITS = [
    0xb904c000, 0xb9adbb00, 0xba556b00, 0xbaf5ce80, 0xbb84b340, 0xbc064180,
    0xbc7e5920, 0xbce15244, 0xbd3a5e80, 0xbd8f926e, 0xbdcd3b48, 0xbe073b54,
    0xbe227686, 0xbe2e0cbe, 0xbe1df8a2, 0xbdcd765d, 0x00000000, 0x3e1944d2,
    0x3eb103af, 0x3f147cd0, 0x3f57625e, 0x3f8f1895, 0x3fb32c4c, 0x3fd706d9,
    0x3ffa2d0c, 0x400e3d5c, 0x401f01a7, 0x402f79be, 0x403fbda6, 0x404fe146,
    0x405ff2aa, 0x406ffa93, 0x407ffded,
]
V_SH = 24.0 + 1.0 / 128.0
Y_SH = 0.25 + 2.0 ** -9
O_SH = 4.0 + 2.0 ** -9
CASC_V = [32.0, 16.0, 8.0, 4.0, 2.0, 1.0, 0.5, 0.25, 0.125, 1 / 16, 1 / 32, 1 / 64]
CASC_W = [4.0, 2.0, 1.0, 0.5, 0.25, 0.125, 1 / 16, 1 / 32, 1 / 64, 1 / 128, 1 / 256]


def _host_consts():
    Y = np.array(PWL_Y_BITS, dtype=np.uint32).view(np.float32).astype(np.float64)
    dY = np.diff(Y)
    A = np.zeros(33)
    A[0] = dY[0]
    A[1:32] = dY[1:] - dY[:-1]
    A[32] = 0.25 - dY[31]
    c_hi = 4.0 - Y[32]
    c_lo = -Y[0]
    c0 = (Y[0] - np.sum(A[1:32] * np.arange(1, 32)) - 32.0 * A[32]
          - 8.0 * np.sum(A) + Y_SH)
    c0 = c0 + np.sum(A[1:13] * (np.arange(1, 13) + 8.0))
    assert np.float64(np.float32(c_hi)) == c_hi
    assert np.float64(np.float32(c_lo)) == c_lo
    targets = (list(A) + [c0, 1e-5]
               + [-float(j + 8) for j in range(1, 13)])
    hi = np.array([np.float32(t) for t in targets], dtype=np.float32)
    lo = np.array([np.float32(t - np.float64(h)) for t, h in zip(targets, hi)],
                  dtype=np.float32)
    return hi, lo, np.float32(c_hi), np.float32(c_lo)


CONST_HI, CONST_LO, C_HI32, C_LO32 = _host_consts()
NCONST = len(CONST_HI)
COL_C0 = 33
COL_EPS = 34

_OPS_INSTALLED = {}


def _mkop(slot, name, body, reference):
    spec = Spec(body=body)
    shas = {v: lower(spec, ver=v, perf_en=False, name=name).sha(v)
            for v in ("v3", "v4")}
    op = DveOp(name, spec, subdim=False, uops_sha=shas, reference=reference)
    old = dve_ops_mod.OPS[slot]
    dve_ops_mod.OPS[slot] = op
    dve_ops_mod._SUB_OPCODE_FOR_NAME.pop(old.name, None)
    dve_ops_mod._SUB_OPCODE_FOR_NAME[name] = dve_ops_mod._CUSTOM_DVE_ROW_BASE + slot
    dve_ops_mod.CUSTOM_DVE_SPECS.pop(old.name, None)
    dve_ops_mod.CUSTOM_DVE_REFS.pop(old.name, None)
    dve_ops_mod.CUSTOM_DVE_SPECS[name] = spec
    dve_ops_mod.CUSTOM_DVE_REFS[name] = reference
    return op


def install_ops():
    if _OPS_INSTALLED:
        return _OPS_INSTALLED
    _r1 = Src0 - C0 * (Src0 >= C0)
    _r2 = _r1 - C1 * (_r1 >= C1)

    def _lvl(r, b):
        return r - b * (r >= b)

    _OPS_INSTALLED["CASC2"] = _mkop(
        0, "XCASC2", _r2,
        lambda in0, in1, s0, s1, imm2: _lvl(_lvl(in0, s0), s1))
    _OPS_INSTALLED["CASC2F"] = _mkop(
        1, "XCASC2F", Src1 - _r2,
        lambda in0, in1, s0, s1, imm2: in1 - _lvl(_lvl(in0, s0), s1))
    _OPS_INSTALLED["KNOT2"] = _mkop(
        2, "XKNOT2", Src1 + C0 * maxx(Src0, C2) + C1 * maxx(Src0, C2 + One),
        lambda in0, in1, s0, s1, imm2:
            in1 + s0 * np.maximum(in0, imm2) + s1 * np.maximum(in0, imm2 + 1.0))
    _body_pinit = C0 + C1 * (Src0 > C2) + C3 * maxx(Src0, C2)
    try:
        from concourse.dve_ops import _spill_c3_to_src1
        _body_pinit = _spill_c3_to_src1(_body_pinit)
    except ImportError:
        pass
    _OPS_INSTALLED["PINITHI"] = _mkop(
        3, "XPINITHI", _body_pinit,
        lambda in0, in1, s0, s1, imm2: s0 + s1 * (in0 > imm2) + in1 * np.maximum(in0, imm2))
    _OPS_INSTALLED["KNOTLO"] = _mkop(
        4, "XKNOTLO", Src1 + C0 * maxx(Src0, C2) + C1 * (C2 > Src0),
        lambda in0, in1, s0, s1, imm2:
            in1 + s0 * np.maximum(in0, imm2) + s1 * (imm2 > in0))
    _OPS_INSTALLED["KNOT1"] = _mkop(
        5, "XKNOT1", Src1 + C0 * maxx(Src0, C1),
        lambda in0, in1, s0, s1, imm2: in1 + s0 * np.maximum(in0, s1))
    _OPS_INSTALLED["CASC1G"] = _mkop(
        6, "XCASC1G", (Src1 - _r1) - C1,
        lambda in0, in1, s0, s1, imm2: (in1 - _lvl(in0, s0)) - s1)
    return _OPS_INSTALLED


ENGS = ("sync", "gpsimd", "scalar", "vector", "tensor")


class Prog:

    def __init__(self, nc):
        self.nc = nc
        self.items = {e: [] for e in ENGS}
        self.count = {e: 0 for e in ENGS}
        self.dma_count = {}
        self.last_write = {}
        self.readers = {}
        self.cur_wait = {e: {} for e in ENGS}

    def emit(self, eng, fn, reads=(), writes=(), dma=None):
        mysem = "s_" + eng if dma is None else None
        waits = {}

        def need(w):
            if w is None:
                return
            s, v = w
            if s.startswith("d_"):
                v = max(v, self.dma_count[s[2:]])
            waits[s] = max(waits.get(s, 0), v)

        for b in reads:
            need(self.last_write.get(b))
        for b in writes:
            need(self.last_write.get(b))
            for s, v in self.readers.get(b, {}).items():
                need((s, v))
        cw = self.cur_wait[eng]
        eff = {}
        for s, v in waits.items():
            if cw.get(s, -1) < v:
                eff[s] = v
                cw[s] = v
        if dma is not None:
            self.dma_count[dma] = self.dma_count.get(dma, 0) + 16
            inc = ("d_" + dma, 16)
            after = ("d_" + dma, self.dma_count[dma])
        else:
            self.count[eng] += 1
            inc = ("s_" + eng, 1)
            after = ("s_" + eng, self.count[eng])
        self.items[eng].append((fn, sorted(eff.items()), inc))
        for b in reads:
            self.readers.setdefault(b, {})
            s, v = after
            self.readers[b][s] = max(self.readers[b].get(s, 0), v)
        for b in writes:
            self.last_write[b] = after
            self.readers[b] = {}
        return after

    def barrier(self, exclude=()):
        tgt = {}
        for e in ENGS:
            if self.count[e]:
                tgt["s_" + e] = self.count[e]
        for st, v in self.dma_count.items():
            if st in exclude:
                continue
            tgt["d_" + st] = v
        for e in ENGS:
            cw = self.cur_wait[e]
            for s, v in tgt.items():
                if s != "s_" + e and cw.get(s, -1) < v:
                    self.items[e].append((None, [(s, v)], None))
                    cw[s] = v


def materialize(nc, prog, sems):
    with nc.Block() as block:
        for eng in ENGS:
            items = prog.items[eng]
            if not items:
                continue

            def mk(eng=eng, items=items):
                def emit_all(engobj):
                    for fn, waits, inc in items:
                        for s, v in waits:
                            engobj.wait_ge(sems[s], v)
                        if fn is None:
                            continue
                        inst = fn(engobj)
                        if inc is not None:
                            inst.then_inc(sems[inc[0]], inc[1])
                return emit_all
            getattr(block, eng)(mk())


def build_nc(T=2048, dbg=False, stop=99):
    OPS = install_ops()
    nc = bass.Bass("TRN2", target_bir_lowering=False)
    TT = T // 128
    GW = min(512, T)
    SUBS = GW // 128
    TG = T // GW
    NQ = max(1, T // 512)
    QW = T // NQ
    QTT = QW // 128

    x_h = nc.dram_tensor("x", [T, D], F32, kind="ExternalInput").ap()
    wq_h = nc.dram_tensor("wq", [D, DQ], F32, kind="ExternalInput").ap()
    wk_h = nc.dram_tensor("wk", [D, DQ], F32, kind="ExternalInput").ap()
    wv_h = nc.dram_tensor("wv", [D, D], F32, kind="ExternalInput").ap()
    w1_h = nc.dram_tensor("w1", [D, FF], F32, kind="ExternalInput").ap()
    w2_h = nc.dram_tensor("w2", [FF, D], F32, kind="ExternalInput").ap()
    b1t_h = nc.dram_tensor("b1t", [128, FF // 128], F32, kind="ExternalInput").ap()
    b2t_h = nc.dram_tensor("b2t", [128, D], F32, kind="ExternalInput").ap()
    chi_h = nc.dram_tensor("chi", [128, NCONST], F32, kind="ExternalInput").ap()
    clo_h = nc.dram_tensor("clo", [128, NCONST], F32, kind="ExternalInput").ap()
    iden_h = nc.dram_tensor("iden", [128, 128], F32, kind="ExternalInput").ap()
    idbf_h = nc.dram_tensor("idbf", [128, 128], BF16, kind="ExternalInput").ap()
    madd_h = nc.dram_tensor("madd", [128, 4 * 512], BF16, kind="ExternalInput").ap()
    maddt_h = nc.dram_tensor("maddt", [128, 4 * 512], BF16, kind="ExternalInput").ap()
    ones_h = nc.dram_tensor("onesr", [1, T], BF16, kind="ExternalInput").ap()
    out_h = nc.dram_tensor("out", [T, D], F32, kind="ExternalOutput").ap()
    w1b_h = nc.dram_tensor("w1b", [FF, D], BF16, kind="ExternalOutput").ap()
    w2b_h = nc.dram_tensor("w2b", [FF, D], BF16, kind="ExternalOutput").ap()
    dbg_h = {}
    if dbg:
        for nm, shp in [("d_xn1fm", [128, T]), ("d_qhat", [65, T]),
                        ("d_khat", [65, T]), ("d_v0", [128, D]),
                        ("d_negm", [128, TT]), ("d_rl", [128, TT]),
                        ("d_x2", [128, D]), ("d_g0", [128, QW]),
                        ("d_oq0", [128, D])]:
            dbg_h[nm] = nc.dram_tensor(nm, shp, F32, kind="ExternalOutput").ap()

    prog = None

    with contextlib.ExitStack() as es:

        def sbuf(name, shape, dtype, st=es):
            return st.enter_context(nc.sbuf_tensor(name, shape, dtype))

        def psum(name, shape, st=es):
            return st.enter_context(nc.psum_tensor(name, shape, F32))

        xs = [sbuf(f"xs{t}", [128, D], F32) for t in range(TT)]
        fm = [sbuf(f"fm{c}", [128, T], BF16) for c in range(D // 128)]
        cst = sbuf("cst", [128, NCONST], F32)
        cst_hi = sbuf("csthi", [128, NCONST], F32)
        cst_lo = sbuf("cstlo", [128, NCONST], F32)
        biasv = sbuf("biasv", [128, FF // 128], F32)
        b1t = sbuf("b1ts", [128, FF // 128], F32)
        iden = sbuf("idens", [128, 128], F32)
        idbf = sbuf("idbfs", [128, 128], BF16)
        negm_cols = sbuf("negmc", [128, TT], F32)
        rl_cols = sbuf("rlc", [128, TT], F32)
        NSL = 4
        stat = [sbuf(f"stat{i}", [128, 12], F32) for i in range(NSL)]
        sc_m4 = [sbuf(f"scm4{i}", [128, 4], F32) for i in range(NSL)]
        sc_s1 = [sbuf(f"scs1{i}", [128, 8], F32) for i in range(NSL)]
        mrow = sbuf("mrow", [TT, 128], BF16)
        es_m = contextlib.ExitStack()
        madd = sbuf("madds", [128, 4 * 512], BF16, st=es_m)
        maddt = sbuf("maddts", [128, 4 * 512], BF16, st=es_m)
        wcs = [sbuf(f"wcs{i}", [128, D], F32, st=es_m) for i in range(2)]
        wcb = [sbuf("wcb0", [128, D], BF16, st=es_m)] * 2

        prog = Prog(nc)
        E = prog.emit

        AP_A = [cst[:, k:k + 1] for k in range(33)]
        AP_C0 = cst[:, COL_C0:COL_C0 + 1]
        AP_EPS = cst[:, COL_EPS:COL_EPS + 1]

        for t in range(TT):
            E("sync", lambda s, t=t: s.dma_start(out=xs[t][:], in_=x_h[t * 128:(t + 1) * 128, :]),
              writes=[f"x{t}"], dma="x")
        for nm, dst, src in [("chi", cst_hi, chi_h), ("clo", cst_lo, clo_h),
                             ("iden", iden, iden_h), ("idbf", idbf, idbf_h), ("madd", madd, madd_h),
                             ("maddt", maddt, maddt_h), ("b1t", b1t, b1t_h)]:
            E("sync", lambda s, dst=dst, src=src: s.dma_start(out=dst[:], in_=src[:, :]),
              writes=[nm], dma="cst")
        E("vector", lambda v: v.tensor_add(cst[:], cst_hi[:], cst_lo[:]),
          reads=["chi", "clo"], writes=["cst"])
        E("vector", lambda v: v.tensor_scalar(biasv[:], b1t[:], 4.0, float(V_SH), AO.mult, AO.add),
          reads=["b1t"], writes=["biasv"])

        _realE = E

        def _sink_E(sink):
            def E2(*a, **kw):
                sink.append(lambda a=a, kw=kw: _realE(*a, **kw))
            return E2

        def ln_stats(src_ap, rkeys, sl, sink=None):
            st_, m4_, s1_ = stat[sl], sc_m4[sl], sc_s1[sl]
            K_ = f"@{sl}"
            if sink is None:
                sink = []
                direct = True
            else:
                direct = False
            E = _sink_E(sink)
            E("vector", lambda v: v.bn_stats(st_[:, 0:6], src_ap[:, 0:512]),
              reads=rkeys, writes=["stat" + K_])
            E("vector", lambda v: v.bn_stats(st_[:, 6:12], src_ap[:, 512:1024]),
              reads=rkeys + ["stat" + K_], writes=["stat" + K_])
            E("vector", lambda v: v.tensor_scalar(m4_[:], st_[:, 1:12:3], 0.25, 0.0,
                                                  AO.mult, AO.add, accum_out=s1_[:, 0:1]),
              reads=["stat" + K_], writes=["s_mean" + K_, "s_m4s" + K_])
            E("vector", lambda v: v.tensor_mul(m4_[:], st_[:, 1:12:3], st_[:, 1:12:3]),
              reads=["stat" + K_, "s_m4s" + K_], writes=["s_m4s" + K_])
            E("vector", lambda v: v.tensor_scalar(m4_[:], m4_[:], 0.25, 0.0,
                                                  AO.mult, AO.add, accum_out=s1_[:, 1:2]),
              reads=["s_m4s" + K_], writes=["s_msq" + K_, "s_m4s" + K_])
            E("vector", lambda v: v.tensor_scalar(m4_[:], st_[:, 2:12:3], 1.0 / 1024.0, 0.0,
                                                  AO.mult, AO.add, accum_out=s1_[:, 2:3]),
              reads=["stat" + K_, "s_m4s" + K_], writes=["s_cv" + K_, "s_m4s" + K_])
            E("vector", lambda v: v.tensor_mul(s1_[:, 3:4], s1_[:, 0:1], s1_[:, 0:1]),
              reads=["s_mean" + K_], writes=["s_meansq" + K_])
            E("vector", lambda v: v.scalar_tensor_tensor(s1_[:, 1:2], s1_[:, 1:2],
                                                         s1_[:, 2:3], s1_[:, 3:4],
                                                         AO.add, AO.subtract),
              reads=["s_msq" + K_, "s_cv" + K_, "s_meansq" + K_], writes=["s_var" + K_])
            E("vector", lambda v: v.tensor_add(s1_[:, 1:2], s1_[:, 1:2], AP_EPS),
              reads=["s_var" + K_, "cst"], writes=["s_ve" + K_])
            E("scalar", lambda sc: sc.activation(s1_[:, 5:6], s1_[:, 1:2], AF.Ln),
              reads=["s_ve" + K_], writes=["s_lnv" + K_])
            E("scalar", lambda sc: sc.activation(s1_[:, 4:5], s1_[:, 5:6], AF.Exp, scale=-0.5),
              reads=["s_lnv" + K_], writes=["s_rstd" + K_])
            if direct:
                for th in sink:
                    th()

        def ln_apply(dst_ap, src_ap, rkeys, wkey, sl, sink=None):
            s1_ = sc_s1[sl]
            K_ = f"@{sl}"
            if sink is None:
                sink = []
                direct = True
            else:
                direct = False
            E = _sink_E(sink)
            E("vector", lambda v: v.tensor_scalar(dst_ap, src_ap, s1_[:, 0:1], s1_[:, 4:5],
                                                  AO.subtract, AO.mult),
              reads=rkeys + ["s_mean" + K_, "s_rstd" + K_], writes=[wkey])
            if direct:
                for th in sink:
                    th()

        def transpose_to_fm(src, srckey, t):
            for c in range(8):
                off = (c % 2) * 512
                E("tensor", lambda te, c=c, off=off: te.transpose(
                    ps_tr[:, off:off + 128],
                    src[:, c * 128:(c + 1) * 128], iden[:]),
                  reads=[srckey, "iden"], writes=[f"tr{c % 2}"])
                E("scalar", lambda sc, c=c, t=t, off=off: sc.activation(
                    fm[c][:, t * 128:(t + 1) * 128],
                    ps_tr[:, off:off + 128], AF.Copy),
                  reads=[f"tr{c % 2}"], writes=[f"fm{c}_{t}"])

        with contextlib.ExitStack() as es1:
            vv = [sbuf(f"vv{t}", [128, D], BF16, st=es1) for t in range(TT)]
            ps_tr = psum("ps_tr", [128, 1024], st=es1)
            khat = sbuf("khat", [65, T], BF16, st=es1)
            qhat = sbuf("qhat", [65, T], BF16, st=es1)
            pbuf = sbuf("pbuf", [128, max(T, TT * GW)], BF16, st=es1)
            xnt = [sbuf(f"xnt{i}", [128, D], F32, st=es1) for i in range(2)]
            asc = [sbuf(f"asca{i}", [128, D], F32, st=es1) for i in range(2)]
            sc_l = [sbuf(f"scl{i}", [128, 1], F32, st=es1) for i in range(2)]
            sc_mb = [sbuf(f"scmb{i}", [128, 1], BF16, st=es1) for i in range(2)]
            wqbf = sbuf("wqbf", [128, DQ * 8], BF16, st=es1)
            wkbf = sbuf("wkbf", [128, DQ * 8], BF16, st=es1)
            wvbf = [sbuf(f"wvbf{k}", [128, D], BF16, st=es1) for k in range(8)]
            es1w = contextlib.ExitStack()
            wq_st = sbuf("wqst", [128, DQ * 4], F32, st=es1w)
            allq = [f"wqst{k}" for k in range(4)]
            for src, dstbf, dkey, dmn in ((wq_h, wqbf, "wqbf", "wqk"),
                                          (wk_h, wkbf, "wkbf", "wqk2")):
                for g4 in range(2):
                    for kc in range(4):
                        E("sync", lambda s, kc=kc, g4=g4, src=src: s.dma_start(
                            out=wq_st[:, kc * DQ:(kc + 1) * DQ],
                            in_=src[(g4 * 4 + kc) * 128:(g4 * 4 + kc + 1) * 128, :]),
                          writes=[f"wqst{kc}"], dma=dmn)
                    E("scalar", lambda sc, g4=g4, dstbf=dstbf, dkey=dkey: sc.activation(
                        dstbf[:, g4 * 256:(g4 + 1) * 256], wq_st[:], AF.Copy),
                      reads=allq, writes=[dkey])
            for kc in range(8):
                for hh in range(4):
                    E("sync", lambda s, kc=kc, hh=hh: s.dma_start(
                        out=wq_st[:], in_=wv_h[kc * 128:(kc + 1) * 128, hh * 256:(hh + 1) * 256]),
                      writes=allq, dma=f"wv{(4 * kc + hh) % 2}")
                    E("scalar", lambda sc, kc=kc, hh=hh: sc.activation(
                        wvbf[kc][:, hh * 256:(hh + 1) * 256], wq_st[:], AF.Copy),
                      reads=allq, writes=[f"wvbf{kc}"])
            es1w.close()

            with contextlib.ExitStack() as es2:
                ps_qk = psum("ps_qk", [64, 512], st=es2)
                ps_v = psum("ps_v", [128, D], st=es2)

                def a_chain(t):
                    pp = t % 2
                    sink = []
                    ln_stats(xs[t][:], [f"x{t}"], pp, sink=sink)
                    if stop >= 2.5:
                        ln_apply(xnt[pp][:], xs[t][:], [f"x{t}"], f"xnt{pp}", pp, sink=sink)
                    return sink
                for t0 in range(0, TT if stop >= 2 else 0, 2):
                    ca = a_chain(t0)
                    cb = a_chain(t0 + 1) if t0 + 1 < TT else []
                    for i in range(max(len(ca), len(cb))):
                        if i < len(ca):
                            ca[i]()
                        if i < len(cb):
                            cb[i]()
                    if stop >= 2.8:
                        transpose_to_fm(xnt[0], "xnt0", t0)
                        if t0 + 1 < TT:
                            transpose_to_fm(xnt[1], "xnt1", t0 + 1)

                wc_steps = []
                for src, dst, kind in ((w2_h, w2b_h, "w2"), (w1_h, w1b_h, "w1")):
                    for fc in range(32 if stop >= 6 else 0):
                        for h in range(2):
                            def d_th(fc=fc, h=h, src=src, kind=kind):
                                qn = "wcs" + ("B" if h else "")
                                eng = "scalar" if h else "sync"
                                if kind == "w2":
                                    E(eng, lambda s: s.dma_start(
                                        out=wcs[h][:, 0:512],
                                        in_=src[fc * 128:(fc + 1) * 128, h * 512:(h + 1) * 512]),
                                      writes=[f"wcs{h}"], dma=qn)
                                else:
                                    for kc in range(4 * h, 4 * h + 4):
                                        E(eng, lambda s, kc=kc: s.dma_start(
                                            out=wcs[h][:, (kc - 4 * h) * 128:(kc - 4 * h + 1) * 128],
                                            in_=src[kc * 128:(kc + 1) * 128, fc * 128:(fc + 1) * 128]),
                                          writes=[f"wcs{h}"], dma=qn)

                            def c_th(fc=fc, h=h, dst=dst, kind=kind):
                                E("scalar", lambda sc: sc.activation(
                                    wcb[0][:, h * 512:(h + 1) * 512], wcs[h][:, 0:512], AF.Copy),
                                  reads=[f"wcs{h}"], writes=["wcb0"])
                                E("gpsimd", lambda g: g.dma_start(
                                    out=dst[fc * 128:(fc + 1) * 128, h * 512:(h + 1) * 512],
                                    in_=wcb[0][:, h * 512:(h + 1) * 512]),
                                  reads=["wcb0"], writes=[kind + "b"],
                                  dma="wco" + ("B" if h else ""))
                            wc_steps.append((d_th, c_th))
                wc_fifo = []

                def wc_unit():
                    if not wc_steps and not wc_fifo:
                        return
                    if len(wc_fifo) >= 2 or not wc_steps:
                        wc_fifo.pop(0)()
                    if wc_steps:
                        d_, c_ = wc_steps.pop(0)
                        d_()
                        wc_fifo.append(c_)

                def wc_pop(n=1):
                    for _ in range(2 * n):
                        wc_unit()

                for tg in range(TG if stop >= 3 else 0):
                    for which, wbf, wkkey, dsts in (
                            ("q", wqbf, "wqbf", qhat), ("k", wkbf, "wkbf", khat)):
                        for kc in range(8):
                            E("tensor", lambda te, kc=kc, tg=tg, wbf=wbf: te.matmul(
                                ps_qk[:, 0:GW], wbf[:, kc * DQ:(kc + 1) * DQ],
                                fm[kc][:, tg * GW:(tg + 1) * GW],
                                start=(kc == 0), stop=(kc == 7)),
                              reads=[wkkey] + [f"fm{kc}_{tg * SUBS + i}" for i in range(SUBS)],
                              writes=["ps_qk"])
                        scale = 0.125 if which == "q" else 1.0
                        E("scalar", lambda sc, tg=tg, dsts=dsts, scale=scale: sc.activation(
                            dsts[0:64, tg * GW:(tg + 1) * GW], ps_qk[:, 0:GW], AF.Copy, scale=scale),
                          reads=["ps_qk"], writes=[f"{which}hat{tg}"])
                for t in range(TT if stop >= 3 else 0):
                    for kc in range(8):
                        for dh in range(2):
                            E("tensor", lambda te, t=t, kc=kc, dh=dh: te.matmul(
                                ps_v[:, dh * 512:(dh + 1) * 512],
                                fm[kc][:, t * 128:(t + 1) * 128],
                                wvbf[kc][:, dh * 512:(dh + 1) * 512],
                                start=(kc == 0), stop=(kc == 7)),
                              reads=[f"fm{kc}_{t}", f"wvbf{kc}"], writes=["ps_v"])
                    E("scalar", lambda sc, t=t: sc.activation(vv[t][:], ps_v[:, :], AF.Copy),
                      reads=["ps_v"], writes=[f"v{t}"])
                    wc_pop(1)
                E("sync", lambda s: s.dma_start(out=khat[64:65, :], in_=ones_h[:, :]),
                  writes=["khat_ones"], dma="ones")

            prog.barrier(exclude=("wcs", "wco"))
            with contextlib.ExitStack() as es2:
                ps_strip = psum("ps_strip", [128, 2048 if T >= 2048 else T], st=es2)

                for qt in range(TT if stop >= 4 else 0):
                    wc_pop(2)
                    klen = (qt + 1) * 128
                    ngr = (klen + GW - 1) // GW
                    for g in range(ngr):
                        n = min(GW, klen - g * GW)
                        last = (g == ngr - 1)
                        E("tensor", lambda te, qt=qt, g=g, n=n, last=last: te.matmul(
                            ps_strip[:, g * GW:g * GW + n],
                            qhat[0:64, qt * 128:(qt + 1) * 128],
                            khat[0:64, g * GW:g * GW + n],
                            start=True, stop=not last),
                          reads=[f"qhat{qt // SUBS}", f"khat{g}"], writes=["ps_strip"])
                    gl = (ngr - 1) * GW
                    n_last = klen - gl
                    var = n_last // 128 - 1
                    E("tensor", lambda te, gl=gl, n_last=n_last, var=var: te.matmul(
                        ps_strip[:, gl:gl + n_last], idbf[:],
                        madd[:, var * 512:var * 512 + n_last],
                        start=False, stop=True),
                      reads=["idbf", "madd"], writes=["ps_strip"])
                    E("vector", lambda v, klen=klen, qt=qt: v.reduce_max(
                        sc_s1[qt % 2][:, 6:7], ps_strip[:, 0:klen], axis=AX),
                      reads=["ps_strip"], writes=[f"s_m@{qt % 2}"])
                    E("vector", lambda v, qt=qt: v.tensor_copy(sc_mb[qt % 2][:], sc_s1[qt % 2][:, 6:7]),
                      reads=[f"s_m@{qt % 2}"], writes=[f"s_mb@{qt % 2}"])
                    E("vector", lambda v, qt=qt: v.tensor_single_scalar(
                        negm_cols[:, qt:qt + 1], sc_mb[qt % 2][:], -1.0, AO.mult),
                      reads=[f"s_mb@{qt % 2}"], writes=[f"negm{qt}"])
                    E("scalar", lambda sc, qt=qt, klen=klen: sc.activation(
                        pbuf[:, 0:klen], ps_strip[:, 0:klen], AF.Exp,
                        bias=negm_cols[:, qt:qt + 1], accum_out=sc_l[qt % 2][:]),
                      reads=["ps_strip", f"negm{qt}"], writes=[f"s_l@{qt % 2}", "pbuf_scr"])
                    E("vector", lambda v, qt=qt: v.reciprocal(rl_cols[:, qt:qt + 1], sc_l[qt % 2][:]),
                      reads=[f"s_l@{qt % 2}"], writes=[f"rl{qt}"])

                if stop >= 4:
                    E("tensor", lambda te: te.transpose(ps_tr[0:TT, 0:128], negm_cols[:], iden[:]),
                      reads=[f"negm{q}" for q in range(TT)] + ["iden"], writes=["tr0"]),
                    E("scalar", lambda sc: sc.activation(mrow[:], ps_tr[0:TT, 0:128], AF.Copy),
                      reads=["tr0"], writes=["mrow"])
                    E("gpsimd", lambda g: g.dma_start(out=qhat[64:65, :], in_=mrow[:]),
                      reads=["mrow"], writes=["qhat_mrow"], dma="mrow")

                if dbg:
                    E("gpsimd", lambda g: g.dma_start(out=dbg_h["d_negm"][:, :], in_=negm_cols[:]),
                      reads=[f"negm{q}" for q in range(TT)], dma="dbg0")
                    E("gpsimd", lambda g: g.dma_start(out=dbg_h["d_rl"][:, :], in_=rl_cols[:]),
                      reads=[f"rl{q}" for q in range(TT)], dma="dbg1")
                    E("gpsimd", lambda g: g.dma_start(out=dbg_h["d_qhat"][:, :], in_=qhat[:]),
                      reads=["qhat_mrow"], dma="dbg2")
                    E("gpsimd", lambda g: g.dma_start(out=dbg_h["d_khat"][:, :], in_=khat[:]),
                      reads=["khat_ones"], dma="dbg3")
                    E("gpsimd", lambda g: g.dma_start(out=dbg_h["d_v0"][:, :], in_=vv[0][:]),
                      reads=["v0"], dma="dbg4")
                    E("gpsimd", lambda g: g.dma_start(out=dbg_h["d_xn1fm"][:, :], in_=fm[0][:]),
                      reads=[f"fm0_{t}" for t in range(TT)], dma="dbg5")

            prog.barrier(exclude=("wcs", "wco"))
            with contextlib.ExitStack() as es2:
                ps_s2 = [psum(f"ps_s2_{i}", [128, 512], st=es2) for i in range(2)]
                ps_a = [psum(f"ps_a{i}", [128, 512], st=es2) for i in range(2)]

                for qg in range(TG if stop >= 5 else 0):
                    wc_pop(2)
                    nk = SUBS * qg + SUBS
                    for kc in range(nk):
                        diag = (kc >= qg * SUBS)
                        sb = kc % 2
                        E("tensor", lambda te, qg=qg, kc=kc, diag=diag, sb=sb: te.matmul(
                            ps_s2[sb][:, 0:GW], khat[0:65, kc * 128:(kc + 1) * 128],
                            qhat[0:65, qg * GW:(qg + 1) * GW],
                            start=True, stop=not diag),
                          reads=[f"khat{kc // SUBS}", "khat_ones",
                                 f"qhat{qg}", "qhat_mrow"], writes=[f"ps_s2_{sb}"])
                        if diag:
                            qs = kc - qg * SUBS
                            E("tensor", lambda te, qs=qs, sb=sb: te.matmul(
                                ps_s2[sb][:, 0:GW], idbf[:],
                                maddt[:, qs * 512:qs * 512 + GW],
                                start=False, stop=True),
                              reads=["idbf", "maddt"], writes=[f"ps_s2_{sb}"])
                        E("scalar", lambda sc, kc=kc, sb=sb: sc.activation(
                            pbuf[:, kc * GW:(kc + 1) * GW], ps_s2[sb][:, 0:GW], AF.Exp),
                          reads=[f"ps_s2_{sb}"], writes=[f"pt{kc}"])
                    def qs_front(qs):
                        qt = qg * SUBS + qs
                        ab = qs % 2
                        for dh in range(2):
                            for kc in range(qt + 1):
                                E("tensor", lambda te, kc=kc, qs=qs, qt=qt, dh=dh, ab=ab: te.matmul(
                                    ps_a[ab][:, :],
                                    pbuf[:, kc * GW + qs * 128: kc * GW + (qs + 1) * 128],
                                    vv[kc][:, dh * 512:(dh + 1) * 512],
                                    start=(kc == 0), stop=(kc == qt)),
                                  reads=[f"pt{kc}", f"v{kc}"], writes=[f"ps_a{ab}"])
                            E("scalar", lambda sc, qt=qt, dh=dh, ab=ab: sc.activation(
                                asc[ab][:, dh * 512:(dh + 1) * 512], ps_a[ab][:, :], AF.Copy,
                                scale=rl_cols[:, qt:qt + 1]),
                              reads=[f"ps_a{ab}", f"rl{qt}"], writes=[f"asc{ab}"])

                    def qs_ln_steps(qs):
                        qt = qg * SUBS + qs
                        ab = qs % 2
                        sink = []
                        ln_stats(asc[ab][:], [f"asc{ab}"], ab, sink=sink)
                        ln_apply(xnt[ab][:], asc[ab][:], [f"asc{ab}"], f"xnt{ab}", ab, sink=sink)
                        sink.append(lambda: E("vector", lambda v, qt=qt, ab=ab: v.tensor_add(
                            xs[qt][:], xs[qt][:], xnt[ab][:]),
                            reads=[f"xnt{ab}", f"x{qt}"], writes=[f"x{qt}"]))
                        ln_stats(xs[qt][:], [f"x{qt}"], 2 + ab, sink=sink)
                        ln_apply(xnt[ab][:], xs[qt][:], [f"x{qt}"], f"xnt{ab}", 2 + ab, sink=sink)
                        return sink

                    for qs0 in range(0, SUBS, 2):
                        wc_pop(1)
                        qs_front(qs0)
                        if qs0 + 1 < SUBS:
                            qs_front(qs0 + 1)
                            a_steps = qs_ln_steps(qs0)
                            b_steps = qs_ln_steps(qs0 + 1)
                            for i in range(max(len(a_steps), len(b_steps))):
                                if i < len(a_steps):
                                    a_steps[i]()
                                if i < len(b_steps):
                                    b_steps[i]()
                            transpose_to_fm(xnt[0], "xnt0", qg * SUBS + qs0)
                            transpose_to_fm(xnt[1], "xnt1", qg * SUBS + qs0 + 1)
                        else:
                            for th in qs_ln_steps(qs0):
                                th()
                            transpose_to_fm(xnt[qs0 % 2], f"xnt{qs0 % 2}", qg * SUBS + qs0)
                if dbg:
                    E("gpsimd", lambda g: g.dma_start(out=dbg_h["d_x2"][:, :], in_=xs[0][:]),
                      reads=["x0"], dma="dbg6")

        while wc_steps or wc_fifo:
            wc_unit()
        prog.barrier()
        es_m.close()
        with contextlib.ExitStack() as es1:
            gq = [sbuf(f"gq{f}", [128, QW], BF16, st=es1) for f in range(32)]
            w1bf = [sbuf(f"w1bf{i}", [128, 1024], BF16, st=es1) for i in range(2)]
            w2bf = [sbuf(f"w2bf{i}", [128, D], BF16, st=es1) for i in range(2)]
            b2sh = sbuf("b2sh", [128, D], F32, st=es1)
            KOFF = list(range(1, 13))
            diagk = [sbuf(f"diagk{j}", [128, 128], BF16, st=es1) for j in KOFF]
            rk = [[sbuf(f"rk{i}_{jj}", [128, QW], mybir.dt.float16, st=es1)
                   for jj in range(2)] for i in range(2)]
            for j in KOFF:
                E("vector", lambda v, j=j: v.tensor_single_scalar(
                    diagk[j - 1][:], idbf[:], AP_A[j], AO.mult),
                  reads=["idbf", "cst"], writes=[f"diagk{j}"])
            E("sync", lambda s: s.dma_start(out=b2sh[:], in_=b2t_h[:, :]),
              writes=["b2t"], dma="b2t")
            E("vector", lambda v: v.tensor_single_scalar(b2sh[:], b2sh[:], float(O_SH), AO.add),
              reads=["b2t"], writes=["b2sh"])
            vsc = [sbuf(f"vsc{i}", [128, QW], F32, st=es1) for i in range(4)]
            rsc = [sbuf(f"rsc{i}", [128, QW], F32, st=es1) for i in range(2)]
            acc = [sbuf(f"acc{i}", [128, QW], F32, st=es1) for i in range(2)]
            wsc = [sbuf(f"wsc{i}", [128, QW], F32, st=es1) for i in range(2)]
            wo = [sbuf(f"wo{i}", [128, D], F32, st=es1) for i in range(2)]
            ro = [sbuf(f"ro{i}", [128, D], F32, st=es1) for i in range(2)]
            oq = ro
            outt = wo

            def casc2(dst, dstk, src, srck, b1, b2):
                E("vector", lambda v, b1=b1, b2=b2: v._custom_dve(
                    OPS["CASC2"], out=dst, in0=src, s0=float(b1), s1=float(b2)),
                  reads=[srck], writes=[dstk])

            ps_h = [psum(f"ps_hh{i}", [128, QW], st=es1) for i in range(2)]
            ps_o = [psum(f"ps_oo{i}", [128, D // 2], st=es1) for i in range(4)]
            ps_y = [psum(f"ps_yy{i}", [128, QW], st=es1) for i in range(2)]
            pending_o = []

            def drain_pending():
                for merged in pending_o:
                    for th in merged:
                        th()
                pending_o.clear()
            for q in range(NQ if stop >= 6 else 0):
                qs_, qe_ = q * QW, (q + 1) * QW
                if True:
                    def g_chain_steps(fc):
                        pb, pv = fc % 2, fc % 4
                        V, R, A_, W = vsc[pv][:], rsc[pb][:], acc[pb][:], wsc[pb][:]
                        kV, kR, kA, kW = f"vsc{pv}", f"rsc{pb}", f"acc{pb}", f"wsc{pb}"
                        steps = []

                        def vcasc(b1, b2, src, srck):
                            steps.append(lambda: E("vector", lambda v: v._custom_dve(
                                OPS["CASC2"], out=R, in0=src, s0=float(b1), s1=float(b2)),
                                reads=[srck], writes=[kR]))
                        vcasc(32.0, 16.0, V, kV)
                        vcasc(8.0, 4.0, R, kR)
                        vcasc(2.0, 1.0, R, kR)
                        vcasc(0.5, 0.25, R, kR)
                        vcasc(0.125, 1 / 16, R, kR)
                        steps.append(lambda: E("vector", lambda v: v._custom_dve(
                            OPS["CASC2F"], out=V, in0=R, in1=V, s0=1 / 32, s1=1 / 64),
                            reads=[kR, kV], writes=[kV]))

                        def sc_knot(j, jj):
                            E("scalar", lambda sc: sc.activation(
                                rk[pb][jj][:], V, AF.Relu, bias=cst[:, 34 + j:35 + j]),
                              reads=[kV, "cst"], writes=[f"rk{pb}_{jj}"])
                            E("tensor", lambda te: te.matmul(
                                ps_y[pb][:, :], diagk[j - 1][:], rk[pb][jj][:],
                                start=(j == KOFF[0]), stop=(j == KOFF[-1])),
                              reads=[f"rk{pb}_{jj}", f"diagk{j}"], writes=[f"ps_y{pb}"])
                        sc_pending = list(KOFF)

                        def with_sc(th):
                            def run():
                                th()
                                if sc_pending:
                                    j = sc_pending.pop(0)
                                    sc_knot(j, j % 2)
                            return run
                        steps.append(with_sc(lambda: E("vector", lambda v: v._custom_dve(
                            OPS["PINITHI"], out=A_, in0=V, in1=AP_A[32],
                            s0=AP_C0, s1=float(C_HI32), imm2=40.0),
                            reads=[kV, "cst"], writes=[kA])))
                        steps.append(with_sc(lambda: E("vector", lambda v: v._custom_dve(
                            OPS["KNOTLO"], out=A_, in0=V, in1=A_,
                            s0=AP_A[0], s1=float(C_LO32), imm2=8.0),
                            reads=[kV, "cst", kA], writes=[kA])))
                        for p in range(6, 15):
                            steps.append(with_sc(lambda p=p: E("vector", lambda v: v._custom_dve(
                                OPS["KNOT2"], out=A_, in0=V, in1=A_,
                                s0=AP_A[1 + 2 * p], s1=AP_A[2 + 2 * p],
                                imm2=float(9 + 2 * p)),
                                reads=[kV, "cst", kA], writes=[kA])))
                        steps.append(with_sc(lambda: E("vector", lambda v: v._custom_dve(
                            OPS["KNOT1"], out=W, in0=V, in1=A_,
                            s0=AP_A[31], s1=39.0),
                            reads=[kV, "cst", kA], writes=[kW])))
                        steps.append(lambda: E("vector", lambda v: v.tensor_add(
                            W, W, ps_y[pb][:, :]),
                            reads=[kW, f"ps_y{pb}"], writes=[kW]))
                        vcasc(4.0, 2.0, W, kW)
                        vcasc(1.0, 0.5, R, kR)
                        vcasc(0.25, 0.125, R, kR)
                        vcasc(1 / 16, 1 / 32, R, kR)
                        vcasc(1 / 64, 1 / 128, R, kR)
                        steps.append(lambda: E("vector", lambda v: v._custom_dve(
                            OPS["CASC1G"], out=gq[fc][:], in0=R, in1=W,
                            s0=1 / 256, s1=0.25),
                            reads=[kR, kW], writes=[f"g{fc}"]))
                        return steps

                    def g_front(fc, qq=None):
                        qq = q if qq is None else qq
                        q2s, q2e = qq * QW, (qq + 1) * QW
                        pb = fc % 2
                        E("sync", lambda s, fc=fc, pb=pb: s.dma_start(
                            out=w1bf[pb][:], in_=w1b_h[fc * 128:(fc + 1) * 128, :]),
                          reads=["w1b"], writes=[f"w1bf{pb}"], dma=f"w1{pb}")
                        for kc in range(8):
                            E("tensor", lambda te, kc=kc, pb=pb, q2s=q2s, q2e=q2e: te.matmul(
                                ps_h[pb][:, :], w1bf[pb][:, kc * 128:(kc + 1) * 128],
                                fm[kc][:, q2s:q2e], start=(kc == 0), stop=(kc == 7)),
                              reads=[f"w1bf{pb}"] + [f"fm{kc}_{qq * QTT + i}" for i in range(QTT)],
                              writes=[f"ps_h{pb}"])
                        E("scalar", lambda sc, fc=fc, pb=pb, pv=fc % 4: sc.activation(
                            vsc[pv][:], ps_h[pb][:, :], AF.Relu, scale=4.0,
                            bias=biasv[:, fc:fc + 1]),
                          reads=[f"ps_h{pb}", "biasv"], writes=[f"vsc{fc % 4}"])

                    def o_mm_fc(half, fc):
                        tts = [t_ for t_ in (2 * half, 2 * half + 1) if t_ < QTT]
                        pb = fc % 2
                        if half == 0:
                            E("gpsimd", lambda g, fc=fc, pb=pb: g.dma_start(
                                out=w2bf[pb][:], in_=w2b_h[fc * 128:(fc + 1) * 128, :]),
                              reads=["w2b"], writes=[f"w2bf{pb}"], dma=f"w2{pb}_0")
                        else:
                            E("sync", lambda s, fc=fc, pb=pb: s.dma_start(
                                out=w2bf[pb][:, 0:512],
                                in_=w2b_h[fc * 128:(fc + 1) * 128, 0:512]),
                              reads=["w2b"], writes=[f"w2bf{pb}"], dma=f"w2{pb}_1")
                            E("gpsimd", lambda g, fc=fc, pb=pb: g.dma_start(
                                out=w2bf[pb][:, 512:1024],
                                in_=w2b_h[fc * 128:(fc + 1) * 128, 512:1024]),
                              reads=["w2b"], writes=[f"w2bf{pb}"], dma=f"w2{pb}_1B")
                        for i, tt in enumerate(tts):
                            for dh in range(2):
                                E("tensor", lambda te, fc=fc, tt=tt, pb=pb, dh=dh, i=i: te.matmul(
                                    ps_o[2 * i + dh][:, :],
                                    gq[fc][:, tt * 128:(tt + 1) * 128],
                                    w2bf[pb][:, dh * 512:(dh + 1) * 512],
                                    start=(fc == 0), stop=(fc == 31)),
                                  reads=[f"g{fc}", f"w2bf{pb}"], writes=[f"ps_o{2 * i + dh}"])

                    if q == 0:
                        for fc4 in range(4):
                            g_front(fc4)
                    for fce in range(0, 32, 2):
                        if fce >= 4:
                            g_front(fce)
                            g_front(fce + 1)
                        if fce == 4:
                            drain_pending()
                        for s0_, s1_ in zip(g_chain_steps(fce), g_chain_steps(fce + 1)):
                            s0_()
                            s1_()
                        if fce >= 4:
                            o_mm_fc(0, fce - 4)
                            o_mm_fc(0, fce - 3)
                    if dbg and q == 0:
                        E("gpsimd", lambda g: g.dma_start(out=dbg_h["d_g0"][:, :], in_=gq[0][:]),
                          reads=["g0"], dma="dbg7")
                    def o_chain(i, tt):
                        t = q * QTT + tt
                        ob = i % 2
                        kWO, kRO = f"wo{ob}", f"ro{ob}"
                        kOQ, kOT = kRO, kWO
                        sink = []
                        E2 = _sink_E(sink)
                        for dh in range(2):
                            E2("vector", lambda v, i=i, ob=ob, dh=dh: v.tensor_add(
                                wo[ob][:, dh * 512:(dh + 1) * 512],
                                ps_o[2 * i + dh][:, :],
                                b2sh[:, dh * 512:(dh + 1) * 512]),
                               reads=[f"ps_o{2 * i + dh}", "b2sh"], writes=[kWO])
                        for b1, b2 in ((4.0, 2.0), (1.0, 0.5), (0.25, 0.125),
                                       (1 / 16, 1 / 32), (1 / 64, 1 / 128)):
                            E2("vector", lambda v, b1=b1, b2=b2, ob=ob, first=(b1 == 4.0): v._custom_dve(
                                OPS["CASC2"], out=ro[ob][:],
                                in0=(wo[ob][:] if first else ro[ob][:]),
                                s0=float(b1), s1=float(b2)),
                               reads=[kWO if b1 == 4.0 else kRO], writes=[kRO])
                        E2("vector", lambda v, ob=ob: v._custom_dve(
                            OPS["CASC1G"], out=oq[ob][:], in0=ro[ob][:], in1=wo[ob][:],
                            s0=1 / 256, s1=4.0),
                           reads=[kRO, kWO], writes=[kOQ])
                        if dbg and t == 0:
                            sink.append(lambda: E("gpsimd", lambda g, ob=ob: g.dma_start(
                                out=dbg_h["d_oq0"][:, :], in_=oq[ob][:]),
                                reads=[kOQ], dma="dbg8"))
                        ln_stats(oq[ob][:], [kOQ], ob, sink=sink)
                        ln_apply(oq[ob][:], oq[ob][:], [kOQ], kOQ, ob, sink=sink)
                        sink.append(lambda: E("vector", lambda v, t=t, ob=ob: v.tensor_add(
                            wo[ob][:], xs[t][:], oq[ob][:]),
                            reads=[f"x{t}", kOQ], writes=[kWO]))
                        sink.append(lambda: E("gpsimd", lambda g, t=t, ob=ob: g.dma_start(
                            out=out_h[t * 128:(t + 1) * 128, :], in_=wo[ob][:]),
                            reads=[kWO], writes=[f"out{t}"], dma="out"))
                        return sink

                    for fc in range(28, 32):
                        o_mm_fc(0, fc)
                    tts0 = [t_ for t_ in (0, 1) if t_ < QTT]
                    chains = [o_chain(i, tt) for i, tt in enumerate(tts0)]
                    for j in range(max(len(c) for c in chains)):
                        for c in chains:
                            if j < len(c):
                                c[j]()
                    if QTT > 2:
                        if q + 1 < NQ:
                            for fc4 in range(4):
                                g_front(fc4, qq=q + 1)
                        for fc in range(32):
                            o_mm_fc(1, fc)
                        chains1 = [o_chain(i, tt) for i, tt in enumerate((2, 3)) if tt < QTT]
                        merged = []
                        for j in range(max(len(c) for c in chains1)):
                            for c in chains1:
                                if j < len(c):
                                    merged.append(c[j])
                        pending_o.append(merged)
            if stop >= 6:
                drain_pending()

        prog.barrier()
        sem_names = (["s_" + e for e in ENGS]
                     + ["d_" + st for st in prog.dma_count])
        sems = {n: es.enter_context(nc.semaphore(n)) for n in sem_names}
        materialize(nc, prog, sems)
    return nc


def _host_inputs(inputs, T):
    wq = np.ascontiguousarray(inputs["L0_attn_Wq"], dtype=np.float32)
    wk = np.ascontiguousarray(inputs["L0_attn_Wk"], dtype=np.float32)
    wv = np.ascontiguousarray(inputs["L0_attn_Wv"], dtype=np.float32)
    w1 = np.ascontiguousarray(inputs["L0_ffn_w1"], dtype=np.float32)
    w2 = np.ascontiguousarray(inputs["L0_ffn_w2"], dtype=np.float32)
    b1 = np.asarray(inputs["L0_ffn_b1"], dtype=np.float32)
    b2 = np.asarray(inputs["L0_ffn_b2"], dtype=np.float32)
    b1t = np.ascontiguousarray(b1.reshape(FF // 128, 128).T)
    b2t = np.ascontiguousarray(np.tile(b2[None, :], (128, 1)))
    chi = np.ascontiguousarray(np.tile(CONST_HI[None, :], (128, 1)))
    clo = np.ascontiguousarray(np.tile(CONST_LO[None, :], (128, 1)))
    iden = np.eye(128, dtype=np.float32)
    q = np.arange(128)[:, None]
    k = np.arange(128)[None, :]
    import ml_dtypes as _md
    m1 = np.where(k <= q, 0.0, -1e30).astype(_md.bfloat16)
    madd = np.zeros((128, 4 * 512), dtype=_md.bfloat16)
    maddt = np.zeros((128, 4 * 512), dtype=_md.bfloat16)
    for v in range(4):
        madd[:, v * 512 + v * 128: v * 512 + (v + 1) * 128] = m1
        maddt[:, v * 512 + v * 128: v * 512 + (v + 1) * 128] = m1.T
    import ml_dtypes
    onesr = np.ones((1, T), dtype=ml_dtypes.bfloat16)
    base = dict(wq=wq, wk=wk, wv=wv, w1=w1, w2=w2, b1t=b1t, b2t=b2t,
                chi=chi, clo=clo, iden=iden, idbf=iden.astype(_md.bfloat16),
                madd=madd, maddt=maddt, onesr=onesr)
    return base


def kernel(L0_attn_X, L0_attn_Wq, L0_attn_Wk, L0_attn_Wv,
           L0_ffn_w1, L0_ffn_b1, L0_ffn_w2, L0_ffn_b2):
    T = int(L0_attn_X.shape[1])
    inputs = dict(L0_attn_X=L0_attn_X, L0_attn_Wq=L0_attn_Wq,
                  L0_attn_Wk=L0_attn_Wk, L0_attn_Wv=L0_attn_Wv,
                  L0_ffn_w1=L0_ffn_w1, L0_ffn_b1=L0_ffn_b1,
                  L0_ffn_w2=L0_ffn_w2, L0_ffn_b2=L0_ffn_b2)
    base = _host_inputs(inputs, T)
    X = np.asarray(L0_attn_X, dtype=np.float32)
    nc = build_nc(T=T)
    lower_extended_insts(nc)
    in_maps = []
    for b in range(NCORE):
        m = dict(base)
        m["x"] = np.ascontiguousarray(X[b])
        in_maps.append(m)
    res = run_bass_kernel_spmd(nc, in_maps, core_ids=list(range(NCORE)))
    out = np.stack([np.asarray(r["out"], dtype=np.float32) for r in res.results])
    return out


# version 26
# speedup vs baseline: 1.1202x; 1.1202x over previous
import contextlib
import numpy as np

import concourse.bass as bass
import concourse.mybir as mybir
from concourse.bass_utils import run_bass_kernel_spmd
from concourse.dve_spec import Spec, Src0, Src1, C0, C1, C2, C3, One, maxx, lower
from concourse.dve_op import DveOp
import concourse.dve_ops as dve_ops_mod
from concourse.library_overlay import lower_extended_insts

F32 = mybir.dt.float32
BF16 = mybir.dt.bfloat16
AO = mybir.AluOpType
AF = mybir.ActivationFunctionType
AX = mybir.AxisListType.X

D = 1024
FF = 4096
DQ = 64
NCORE = 8

PWL_Y_BITS = [
    0xb904c000, 0xb9adbb00, 0xba556b00, 0xbaf5ce80, 0xbb84b340, 0xbc064180,
    0xbc7e5920, 0xbce15244, 0xbd3a5e80, 0xbd8f926e, 0xbdcd3b48, 0xbe073b54,
    0xbe227686, 0xbe2e0cbe, 0xbe1df8a2, 0xbdcd765d, 0x00000000, 0x3e1944d2,
    0x3eb103af, 0x3f147cd0, 0x3f57625e, 0x3f8f1895, 0x3fb32c4c, 0x3fd706d9,
    0x3ffa2d0c, 0x400e3d5c, 0x401f01a7, 0x402f79be, 0x403fbda6, 0x404fe146,
    0x405ff2aa, 0x406ffa93, 0x407ffded,
]
V_SH = 24.0 + 1.0 / 128.0
Y_SH = 0.25 + 2.0 ** -9
O_SH = 4.0 + 2.0 ** -9
CASC_V = [32.0, 16.0, 8.0, 4.0, 2.0, 1.0, 0.5, 0.25, 0.125, 1 / 16, 1 / 32, 1 / 64]
CASC_W = [4.0, 2.0, 1.0, 0.5, 0.25, 0.125, 1 / 16, 1 / 32, 1 / 64, 1 / 128, 1 / 256]


def _host_consts():
    Y = np.array(PWL_Y_BITS, dtype=np.uint32).view(np.float32).astype(np.float64)
    dY = np.diff(Y)
    A = np.zeros(33)
    A[0] = dY[0]
    A[1:32] = dY[1:] - dY[:-1]
    A[32] = 0.25 - dY[31]
    c_hi = 4.0 - Y[32]
    c_lo = -Y[0]
    c0 = (Y[0] - np.sum(A[1:32] * np.arange(1, 32)) - 32.0 * A[32]
          - 8.0 * np.sum(A) + Y_SH)
    c0 = c0 + np.sum(A[1:13] * (np.arange(1, 13) + 8.0))
    assert np.float64(np.float32(c_hi)) == c_hi
    assert np.float64(np.float32(c_lo)) == c_lo
    targets = (list(A) + [c0, 1e-5]
               + [-float(j + 8) for j in range(1, 13)])
    hi = np.array([np.float32(t) for t in targets], dtype=np.float32)
    lo = np.array([np.float32(t - np.float64(h)) for t, h in zip(targets, hi)],
                  dtype=np.float32)
    return hi, lo, np.float32(c_hi), np.float32(c_lo)


CONST_HI, CONST_LO, C_HI32, C_LO32 = _host_consts()
NCONST = len(CONST_HI)
COL_C0 = 33
COL_EPS = 34

_OPS_INSTALLED = {}


def _mkop(slot, name, body, reference):
    spec = Spec(body=body)
    shas = {v: lower(spec, ver=v, perf_en=False, name=name).sha(v)
            for v in ("v3", "v4")}
    op = DveOp(name, spec, subdim=False, uops_sha=shas, reference=reference)
    old = dve_ops_mod.OPS[slot]
    dve_ops_mod.OPS[slot] = op
    dve_ops_mod._SUB_OPCODE_FOR_NAME.pop(old.name, None)
    dve_ops_mod._SUB_OPCODE_FOR_NAME[name] = dve_ops_mod._CUSTOM_DVE_ROW_BASE + slot
    dve_ops_mod.CUSTOM_DVE_SPECS.pop(old.name, None)
    dve_ops_mod.CUSTOM_DVE_REFS.pop(old.name, None)
    dve_ops_mod.CUSTOM_DVE_SPECS[name] = spec
    dve_ops_mod.CUSTOM_DVE_REFS[name] = reference
    return op


def install_ops():
    if _OPS_INSTALLED:
        return _OPS_INSTALLED
    _r1 = Src0 - C0 * (Src0 >= C0)
    _r2 = _r1 - C1 * (_r1 >= C1)

    def _lvl(r, b):
        return r - b * (r >= b)

    _OPS_INSTALLED["CASC2"] = _mkop(
        0, "XCASC2", _r2,
        lambda in0, in1, s0, s1, imm2: _lvl(_lvl(in0, s0), s1))
    _OPS_INSTALLED["CASC2F"] = _mkop(
        1, "XCASC2F", Src1 - _r2,
        lambda in0, in1, s0, s1, imm2: in1 - _lvl(_lvl(in0, s0), s1))
    _OPS_INSTALLED["KNOT2"] = _mkop(
        2, "XKNOT2", Src1 + C0 * maxx(Src0, C2) + C1 * maxx(Src0, C2 + One),
        lambda in0, in1, s0, s1, imm2:
            in1 + s0 * np.maximum(in0, imm2) + s1 * np.maximum(in0, imm2 + 1.0))
    _body_pinit = C0 + C1 * (Src0 > C2) + C3 * maxx(Src0, C2)
    try:
        from concourse.dve_ops import _spill_c3_to_src1
        _body_pinit = _spill_c3_to_src1(_body_pinit)
    except ImportError:
        pass
    _OPS_INSTALLED["PINITHI"] = _mkop(
        3, "XPINITHI", _body_pinit,
        lambda in0, in1, s0, s1, imm2: s0 + s1 * (in0 > imm2) + in1 * np.maximum(in0, imm2))
    _OPS_INSTALLED["KNOTLO"] = _mkop(
        4, "XKNOTLO", Src1 + C0 * maxx(Src0, C2) + C1 * (C2 > Src0),
        lambda in0, in1, s0, s1, imm2:
            in1 + s0 * np.maximum(in0, imm2) + s1 * (imm2 > in0))
    _OPS_INSTALLED["KNOT1"] = _mkop(
        5, "XKNOT1", Src1 + C0 * maxx(Src0, C1),
        lambda in0, in1, s0, s1, imm2: in1 + s0 * np.maximum(in0, s1))
    _OPS_INSTALLED["CASC1G"] = _mkop(
        6, "XCASC1G", (Src1 - _r1) - C1,
        lambda in0, in1, s0, s1, imm2: (in1 - _lvl(in0, s0)) - s1)
    return _OPS_INSTALLED


ENGS = ("sync", "gpsimd", "scalar", "vector", "tensor")


class Prog:

    def __init__(self, nc):
        self.nc = nc
        self.items = {e: [] for e in ENGS}
        self.count = {e: 0 for e in ENGS}
        self.dma_count = {}
        self.last_write = {}
        self.readers = {}
        self.cur_wait = {e: {} for e in ENGS}

    def emit(self, eng, fn, reads=(), writes=(), dma=None):
        mysem = "s_" + eng if dma is None else None
        waits = {}

        def need(w):
            if w is None:
                return
            s, v = w
            if s.startswith("d_"):
                v = max(v, self.dma_count[s[2:]])
            waits[s] = max(waits.get(s, 0), v)

        for b in reads:
            need(self.last_write.get(b))
        for b in writes:
            need(self.last_write.get(b))
            for s, v in self.readers.get(b, {}).items():
                need((s, v))
        cw = self.cur_wait[eng]
        eff = {}
        for s, v in waits.items():
            if cw.get(s, -1) < v:
                eff[s] = v
                cw[s] = v
        if dma is not None:
            self.dma_count[dma] = self.dma_count.get(dma, 0) + 16
            inc = ("d_" + dma, 16)
            after = ("d_" + dma, self.dma_count[dma])
        else:
            self.count[eng] += 1
            inc = ("s_" + eng, 1)
            after = ("s_" + eng, self.count[eng])
        self.items[eng].append((fn, sorted(eff.items()), inc))
        for b in reads:
            self.readers.setdefault(b, {})
            s, v = after
            self.readers[b][s] = max(self.readers[b].get(s, 0), v)
        for b in writes:
            self.last_write[b] = after
            self.readers[b] = {}
        return after

    def barrier(self):
        tgt = {}
        for e in ENGS:
            if self.count[e]:
                tgt["s_" + e] = self.count[e]
        for st, v in self.dma_count.items():
            tgt["d_" + st] = v
        for e in ENGS:
            cw = self.cur_wait[e]
            for s, v in tgt.items():
                if s != "s_" + e and cw.get(s, -1) < v:
                    self.items[e].append((None, [(s, v)], None))
                    cw[s] = v


def materialize(nc, prog, sems):
    with nc.Block() as block:
        for eng in ENGS:
            items = prog.items[eng]
            if not items:
                continue

            def mk(eng=eng, items=items):
                def emit_all(engobj):
                    for fn, waits, inc in items:
                        for s, v in waits:
                            engobj.wait_ge(sems[s], v)
                        if fn is None:
                            continue
                        inst = fn(engobj)
                        if inc is not None:
                            inst.then_inc(sems[inc[0]], inc[1])
                return emit_all
            getattr(block, eng)(mk())


def build_nc(T=2048, dbg=False, stop=99):
    OPS = install_ops()
    nc = bass.Bass("TRN2", target_bir_lowering=False)
    TT = T // 128
    GW = min(512, T)
    SUBS = GW // 128
    TG = T // GW
    NQ = max(1, T // 512)
    QW = T // NQ
    QTT = QW // 128

    x_h = nc.dram_tensor("x", [T, D], F32, kind="ExternalInput").ap()
    wq_h = nc.dram_tensor("wq", [D, DQ], F32, kind="ExternalInput").ap()
    wk_h = nc.dram_tensor("wk", [D, DQ], F32, kind="ExternalInput").ap()
    wv_h = nc.dram_tensor("wv", [D, D], F32, kind="ExternalInput").ap()
    w1_h = nc.dram_tensor("w1", [D, FF], F32, kind="ExternalInput").ap()
    w2_h = nc.dram_tensor("w2", [FF, D], F32, kind="ExternalInput").ap()
    b1t_h = nc.dram_tensor("b1t", [128, FF // 128], F32, kind="ExternalInput").ap()
    b2t_h = nc.dram_tensor("b2t", [128, D], F32, kind="ExternalInput").ap()
    chi_h = nc.dram_tensor("chi", [128, NCONST], F32, kind="ExternalInput").ap()
    clo_h = nc.dram_tensor("clo", [128, NCONST], F32, kind="ExternalInput").ap()
    iden_h = nc.dram_tensor("iden", [128, 128], F32, kind="ExternalInput").ap()
    idbf_h = nc.dram_tensor("idbf", [128, 128], BF16, kind="ExternalInput").ap()
    madd_h = nc.dram_tensor("madd", [128, 4 * 512], BF16, kind="ExternalInput").ap()
    maddt_h = nc.dram_tensor("maddt", [128, 4 * 512], BF16, kind="ExternalInput").ap()
    ones_h = nc.dram_tensor("onesr", [1, T], BF16, kind="ExternalInput").ap()
    out_h = nc.dram_tensor("out", [T, D], F32, kind="ExternalOutput").ap()
    dbg_h = {}
    if dbg:
        for nm, shp in [("d_xn1fm", [128, T]), ("d_qhat", [65, T]),
                        ("d_khat", [65, T]), ("d_v0", [128, D]),
                        ("d_negm", [128, TT]), ("d_rl", [128, TT]),
                        ("d_x2", [128, D]), ("d_g0", [128, QW]),
                        ("d_oq0", [128, D])]:
            dbg_h[nm] = nc.dram_tensor(nm, shp, F32, kind="ExternalOutput").ap()

    prog = None

    with contextlib.ExitStack() as es:

        def sbuf(name, shape, dtype, st=es):
            return st.enter_context(nc.sbuf_tensor(name, shape, dtype))

        def psum(name, shape, st=es):
            return st.enter_context(nc.psum_tensor(name, shape, F32))

        xs = [sbuf(f"xs{t}", [128, D], F32) for t in range(TT)]
        fm = [sbuf(f"fm{c}", [128, T], BF16) for c in range(D // 128)]
        cst = sbuf("cst", [128, NCONST], F32)
        cst_hi = sbuf("csthi", [128, NCONST], F32)
        cst_lo = sbuf("cstlo", [128, NCONST], F32)
        biasv = sbuf("biasv", [128, FF // 128], F32)
        b1t = sbuf("b1ts", [128, FF // 128], F32)
        iden = sbuf("idens", [128, 128], F32)
        idbf = sbuf("idbfs", [128, 128], BF16)
        madd = sbuf("madds", [128, 4 * 512], BF16)
        maddt = sbuf("maddts", [128, 4 * 512], BF16)
        negm_cols = sbuf("negmc", [128, TT], F32)
        rl_cols = sbuf("rlc", [128, TT], F32)
        NSL = 4
        stat = [sbuf(f"stat{i}", [128, 12], F32) for i in range(NSL)]
        sc_m4 = [sbuf(f"scm4{i}", [128, 4], F32) for i in range(NSL)]
        sc_s1 = [sbuf(f"scs1{i}", [128, 8], F32) for i in range(NSL)]
        mrow = sbuf("mrow", [TT, 128], BF16)

        prog = Prog(nc)
        E = prog.emit

        AP_A = [cst[:, k:k + 1] for k in range(33)]
        AP_C0 = cst[:, COL_C0:COL_C0 + 1]
        AP_EPS = cst[:, COL_EPS:COL_EPS + 1]

        for t in range(TT):
            E("sync", lambda s, t=t: s.dma_start(out=xs[t][:], in_=x_h[t * 128:(t + 1) * 128, :]),
              writes=[f"x{t}"], dma="x")
        for nm, dst, src in [("chi", cst_hi, chi_h), ("clo", cst_lo, clo_h),
                             ("iden", iden, iden_h), ("idbf", idbf, idbf_h), ("madd", madd, madd_h),
                             ("maddt", maddt, maddt_h), ("b1t", b1t, b1t_h)]:
            E("sync", lambda s, dst=dst, src=src: s.dma_start(out=dst[:], in_=src[:, :]),
              writes=[nm], dma="cst")
        E("vector", lambda v: v.tensor_add(cst[:], cst_hi[:], cst_lo[:]),
          reads=["chi", "clo"], writes=["cst"])
        E("vector", lambda v: v.tensor_scalar(biasv[:], b1t[:], 4.0, float(V_SH), AO.mult, AO.add),
          reads=["b1t"], writes=["biasv"])

        _realE = E

        def _sink_E(sink):
            def E2(*a, **kw):
                sink.append(lambda a=a, kw=kw: _realE(*a, **kw))
            return E2

        def ln_stats(src_ap, rkeys, sl, sink=None):
            st_, m4_, s1_ = stat[sl], sc_m4[sl], sc_s1[sl]
            K_ = f"@{sl}"
            if sink is None:
                sink = []
                direct = True
            else:
                direct = False
            E = _sink_E(sink)
            E("vector", lambda v: v.bn_stats(st_[:, 0:6], src_ap[:, 0:512]),
              reads=rkeys, writes=["stat" + K_])
            E("vector", lambda v: v.bn_stats(st_[:, 6:12], src_ap[:, 512:1024]),
              reads=rkeys + ["stat" + K_], writes=["stat" + K_])
            E("vector", lambda v: v.tensor_scalar(m4_[:], st_[:, 1:12:3], 0.25, 0.0,
                                                  AO.mult, AO.add, accum_out=s1_[:, 0:1]),
              reads=["stat" + K_], writes=["s_mean" + K_, "s_m4s" + K_])
            E("vector", lambda v: v.tensor_mul(m4_[:], st_[:, 1:12:3], st_[:, 1:12:3]),
              reads=["stat" + K_, "s_m4s" + K_], writes=["s_m4s" + K_])
            E("vector", lambda v: v.tensor_scalar(m4_[:], m4_[:], 0.25, 0.0,
                                                  AO.mult, AO.add, accum_out=s1_[:, 1:2]),
              reads=["s_m4s" + K_], writes=["s_msq" + K_, "s_m4s" + K_])
            E("vector", lambda v: v.tensor_scalar(m4_[:], st_[:, 2:12:3], 1.0 / 1024.0, 0.0,
                                                  AO.mult, AO.add, accum_out=s1_[:, 2:3]),
              reads=["stat" + K_, "s_m4s" + K_], writes=["s_cv" + K_, "s_m4s" + K_])
            E("vector", lambda v: v.tensor_mul(s1_[:, 3:4], s1_[:, 0:1], s1_[:, 0:1]),
              reads=["s_mean" + K_], writes=["s_meansq" + K_])
            E("vector", lambda v: v.scalar_tensor_tensor(s1_[:, 1:2], s1_[:, 1:2],
                                                         s1_[:, 2:3], s1_[:, 3:4],
                                                         AO.add, AO.subtract),
              reads=["s_msq" + K_, "s_cv" + K_, "s_meansq" + K_], writes=["s_var" + K_])
            E("vector", lambda v: v.tensor_add(s1_[:, 1:2], s1_[:, 1:2], AP_EPS),
              reads=["s_var" + K_, "cst"], writes=["s_ve" + K_])
            E("scalar", lambda sc: sc.activation(s1_[:, 5:6], s1_[:, 1:2], AF.Ln),
              reads=["s_ve" + K_], writes=["s_lnv" + K_])
            E("scalar", lambda sc: sc.activation(s1_[:, 4:5], s1_[:, 5:6], AF.Exp, scale=-0.5),
              reads=["s_lnv" + K_], writes=["s_rstd" + K_])
            if direct:
                for th in sink:
                    th()

        def ln_apply(dst_ap, src_ap, rkeys, wkey, sl, sink=None):
            s1_ = sc_s1[sl]
            K_ = f"@{sl}"
            if sink is None:
                sink = []
                direct = True
            else:
                direct = False
            E = _sink_E(sink)
            E("vector", lambda v: v.tensor_scalar(dst_ap, src_ap, s1_[:, 0:1], s1_[:, 4:5],
                                                  AO.subtract, AO.mult),
              reads=rkeys + ["s_mean" + K_, "s_rstd" + K_], writes=[wkey])
            if direct:
                for th in sink:
                    th()

        def transpose_to_fm(src, srckey, t):
            for c in range(8):
                off = (c % 2) * 512
                E("tensor", lambda te, c=c, off=off: te.transpose(
                    ps_tr[:, off:off + 128],
                    src[:, c * 128:(c + 1) * 128], iden[:]),
                  reads=[srckey, "iden"], writes=[f"tr{c % 2}"])
                E("scalar", lambda sc, c=c, t=t, off=off: sc.activation(
                    fm[c][:, t * 128:(t + 1) * 128],
                    ps_tr[:, off:off + 128], AF.Copy),
                  reads=[f"tr{c % 2}"], writes=[f"fm{c}_{t}"])

        with contextlib.ExitStack() as es1:
            vv = [sbuf(f"vv{t}", [128, D], BF16, st=es1) for t in range(TT)]
            ps_tr = psum("ps_tr", [128, 1024], st=es1)
            khat = sbuf("khat", [65, T], BF16, st=es1)
            qhat = sbuf("qhat", [65, T], BF16, st=es1)
            pbuf = sbuf("pbuf", [128, max(T, TT * GW)], BF16, st=es1)
            xnt = [sbuf(f"xnt{i}", [128, D], F32, st=es1) for i in range(2)]
            asc = [sbuf(f"asca{i}", [128, D], F32, st=es1) for i in range(2)]
            sc_l = [sbuf(f"scl{i}", [128, 1], F32, st=es1) for i in range(2)]
            sc_mb = [sbuf(f"scmb{i}", [128, 1], BF16, st=es1) for i in range(2)]
            wqbf = sbuf("wqbf", [128, DQ * 8], BF16, st=es1)
            wkbf = sbuf("wkbf", [128, DQ * 8], BF16, st=es1)
            wvbf = [sbuf(f"wvbf{k}", [128, D], BF16, st=es1) for k in range(8)]
            es1w = contextlib.ExitStack()
            wq_st = sbuf("wqst", [128, DQ * 8], F32, st=es1w)
            wk_st = wq_st
            wv_st = [sbuf(f"wvst{i}", [128, D], F32, st=es1w) for i in range(2)]
            for kc in range(8):
                E("sync", lambda s, kc=kc: s.dma_start(
                    out=wq_st[:, kc * DQ:(kc + 1) * DQ], in_=wq_h[kc * 128:(kc + 1) * 128, :]),
                  writes=[f"wqst{kc}"], dma="wqk")
            E("scalar", lambda sc: sc.activation(wqbf[:], wq_st[:], AF.Copy),
              reads=[f"wqst{k}" for k in range(8)], writes=["wqbf"])
            for kc in range(8):
                E("sync", lambda s, kc=kc: s.dma_start(
                    out=wk_st[:, kc * DQ:(kc + 1) * DQ], in_=wk_h[kc * 128:(kc + 1) * 128, :]),
                  writes=[f"wqst{kc}"], dma="wqk2")
            E("scalar", lambda sc: sc.activation(wkbf[:], wk_st[:], AF.Copy),
              reads=[f"wqst{k}" for k in range(8)], writes=["wkbf"])
            for kc in range(8):
                st = wv_st[kc % 2]
                E("sync", lambda s, kc=kc, st=st: s.dma_start(
                    out=st[:], in_=wv_h[kc * 128:(kc + 1) * 128, :]),
                  reads=[], writes=[f"wvst{kc % 2}"], dma=f"wv{kc % 2}")
                E("scalar", lambda sc, kc=kc, st=st: sc.activation(wvbf[kc][:], st[:], AF.Copy),
                  reads=[f"wvst{kc % 2}"], writes=[f"wvbf{kc}"])
            es1w.close()

            with contextlib.ExitStack() as es2:
                ps_qk = psum("ps_qk", [64, 512], st=es2)
                ps_v = psum("ps_v", [128, D], st=es2)

                def a_chain(t):
                    pp = t % 2
                    sink = []
                    ln_stats(xs[t][:], [f"x{t}"], pp, sink=sink)
                    if stop >= 2.5:
                        ln_apply(xnt[pp][:], xs[t][:], [f"x{t}"], f"xnt{pp}", pp, sink=sink)
                    return sink
                for t0 in range(0, TT if stop >= 2 else 0, 2):
                    ca = a_chain(t0)
                    cb = a_chain(t0 + 1) if t0 + 1 < TT else []
                    for i in range(max(len(ca), len(cb))):
                        if i < len(ca):
                            ca[i]()
                        if i < len(cb):
                            cb[i]()
                    if stop >= 2.8:
                        transpose_to_fm(xnt[0], "xnt0", t0)
                        if t0 + 1 < TT:
                            transpose_to_fm(xnt[1], "xnt1", t0 + 1)

                for tg in range(TG if stop >= 3 else 0):
                    for which, wbf, wkkey, dsts in (
                            ("q", wqbf, "wqbf", qhat), ("k", wkbf, "wkbf", khat)):
                        for kc in range(8):
                            E("tensor", lambda te, kc=kc, tg=tg, wbf=wbf: te.matmul(
                                ps_qk[:, 0:GW], wbf[:, kc * DQ:(kc + 1) * DQ],
                                fm[kc][:, tg * GW:(tg + 1) * GW],
                                start=(kc == 0), stop=(kc == 7)),
                              reads=[wkkey] + [f"fm{kc}_{tg * SUBS + i}" for i in range(SUBS)],
                              writes=["ps_qk"])
                        scale = 0.125 if which == "q" else 1.0
                        E("scalar", lambda sc, tg=tg, dsts=dsts, scale=scale: sc.activation(
                            dsts[0:64, tg * GW:(tg + 1) * GW], ps_qk[:, 0:GW], AF.Copy, scale=scale),
                          reads=["ps_qk"], writes=[f"{which}hat{tg}"])
                for t in range(TT if stop >= 3 else 0):
                    for kc in range(8):
                        for dh in range(2):
                            E("tensor", lambda te, t=t, kc=kc, dh=dh: te.matmul(
                                ps_v[:, dh * 512:(dh + 1) * 512],
                                fm[kc][:, t * 128:(t + 1) * 128],
                                wvbf[kc][:, dh * 512:(dh + 1) * 512],
                                start=(kc == 0), stop=(kc == 7)),
                              reads=[f"fm{kc}_{t}", f"wvbf{kc}"], writes=["ps_v"])
                    E("scalar", lambda sc, t=t: sc.activation(vv[t][:], ps_v[:, :], AF.Copy),
                      reads=["ps_v"], writes=[f"v{t}"])
                E("sync", lambda s: s.dma_start(out=khat[64:65, :], in_=ones_h[:, :]),
                  writes=["khat_ones"], dma="ones")

            prog.barrier()
            with contextlib.ExitStack() as es2:
                ps_strip = psum("ps_strip", [128, 2048 if T >= 2048 else T], st=es2)

                for qt in range(TT if stop >= 4 else 0):
                    klen = (qt + 1) * 128
                    ngr = (klen + GW - 1) // GW
                    for g in range(ngr):
                        n = min(GW, klen - g * GW)
                        last = (g == ngr - 1)
                        E("tensor", lambda te, qt=qt, g=g, n=n, last=last: te.matmul(
                            ps_strip[:, g * GW:g * GW + n],
                            qhat[0:64, qt * 128:(qt + 1) * 128],
                            khat[0:64, g * GW:g * GW + n],
                            start=True, stop=not last),
                          reads=[f"qhat{qt // SUBS}", f"khat{g}"], writes=["ps_strip"])
                    gl = (ngr - 1) * GW
                    n_last = klen - gl
                    var = n_last // 128 - 1
                    E("tensor", lambda te, gl=gl, n_last=n_last, var=var: te.matmul(
                        ps_strip[:, gl:gl + n_last], idbf[:],
                        madd[:, var * 512:var * 512 + n_last],
                        start=False, stop=True),
                      reads=["idbf", "madd"], writes=["ps_strip"])
                    E("vector", lambda v, klen=klen, qt=qt: v.reduce_max(
                        sc_s1[qt % 2][:, 6:7], ps_strip[:, 0:klen], axis=AX),
                      reads=["ps_strip"], writes=[f"s_m@{qt % 2}"])
                    E("vector", lambda v, qt=qt: v.tensor_copy(sc_mb[qt % 2][:], sc_s1[qt % 2][:, 6:7]),
                      reads=[f"s_m@{qt % 2}"], writes=[f"s_mb@{qt % 2}"])
                    E("vector", lambda v, qt=qt: v.tensor_single_scalar(
                        negm_cols[:, qt:qt + 1], sc_mb[qt % 2][:], -1.0, AO.mult),
                      reads=[f"s_mb@{qt % 2}"], writes=[f"negm{qt}"])
                    E("scalar", lambda sc, qt=qt, klen=klen: sc.activation(
                        pbuf[:, 0:klen], ps_strip[:, 0:klen], AF.Exp,
                        bias=negm_cols[:, qt:qt + 1], accum_out=sc_l[qt % 2][:]),
                      reads=["ps_strip", f"negm{qt}"], writes=[f"s_l@{qt % 2}", "pbuf_scr"])
                    E("vector", lambda v, qt=qt: v.reciprocal(rl_cols[:, qt:qt + 1], sc_l[qt % 2][:]),
                      reads=[f"s_l@{qt % 2}"], writes=[f"rl{qt}"])

                if stop >= 4:
                    E("tensor", lambda te: te.transpose(ps_tr[0:TT, 0:128], negm_cols[:], iden[:]),
                      reads=[f"negm{q}" for q in range(TT)] + ["iden"], writes=["tr0"]),
                    E("scalar", lambda sc: sc.activation(mrow[:], ps_tr[0:TT, 0:128], AF.Copy),
                      reads=["tr0"], writes=["mrow"])
                    E("gpsimd", lambda g: g.dma_start(out=qhat[64:65, :], in_=mrow[:]),
                      reads=["mrow"], writes=["qhat_mrow"], dma="mrow")

                if dbg:
                    E("gpsimd", lambda g: g.dma_start(out=dbg_h["d_negm"][:, :], in_=negm_cols[:]),
                      reads=[f"negm{q}" for q in range(TT)], dma="dbg0")
                    E("gpsimd", lambda g: g.dma_start(out=dbg_h["d_rl"][:, :], in_=rl_cols[:]),
                      reads=[f"rl{q}" for q in range(TT)], dma="dbg1")
                    E("gpsimd", lambda g: g.dma_start(out=dbg_h["d_qhat"][:, :], in_=qhat[:]),
                      reads=["qhat_mrow"], dma="dbg2")
                    E("gpsimd", lambda g: g.dma_start(out=dbg_h["d_khat"][:, :], in_=khat[:]),
                      reads=["khat_ones"], dma="dbg3")
                    E("gpsimd", lambda g: g.dma_start(out=dbg_h["d_v0"][:, :], in_=vv[0][:]),
                      reads=["v0"], dma="dbg4")
                    E("gpsimd", lambda g: g.dma_start(out=dbg_h["d_xn1fm"][:, :], in_=fm[0][:]),
                      reads=[f"fm0_{t}" for t in range(TT)], dma="dbg5")

            prog.barrier()
            with contextlib.ExitStack() as es2:
                ps_s2 = [psum(f"ps_s2_{i}", [128, 512], st=es2) for i in range(2)]
                ps_a = [psum(f"ps_a{i}", [128, 512], st=es2) for i in range(2)]

                for qg in range(TG if stop >= 5 else 0):
                    nk = SUBS * qg + SUBS
                    for kc in range(nk):
                        diag = (kc >= qg * SUBS)
                        sb = kc % 2
                        E("tensor", lambda te, qg=qg, kc=kc, diag=diag, sb=sb: te.matmul(
                            ps_s2[sb][:, 0:GW], khat[0:65, kc * 128:(kc + 1) * 128],
                            qhat[0:65, qg * GW:(qg + 1) * GW],
                            start=True, stop=not diag),
                          reads=[f"khat{kc // SUBS}", "khat_ones",
                                 f"qhat{qg}", "qhat_mrow"], writes=[f"ps_s2_{sb}"])
                        if diag:
                            qs = kc - qg * SUBS
                            E("tensor", lambda te, qs=qs, sb=sb: te.matmul(
                                ps_s2[sb][:, 0:GW], idbf[:],
                                maddt[:, qs * 512:qs * 512 + GW],
                                start=False, stop=True),
                              reads=["idbf", "maddt"], writes=[f"ps_s2_{sb}"])
                        E("scalar", lambda sc, kc=kc, sb=sb: sc.activation(
                            pbuf[:, kc * GW:(kc + 1) * GW], ps_s2[sb][:, 0:GW], AF.Exp),
                          reads=[f"ps_s2_{sb}"], writes=[f"pt{kc}"])
                    def qs_front(qs):
                        qt = qg * SUBS + qs
                        ab = qs % 2
                        for dh in range(2):
                            for kc in range(qt + 1):
                                E("tensor", lambda te, kc=kc, qs=qs, qt=qt, dh=dh, ab=ab: te.matmul(
                                    ps_a[ab][:, :],
                                    pbuf[:, kc * GW + qs * 128: kc * GW + (qs + 1) * 128],
                                    vv[kc][:, dh * 512:(dh + 1) * 512],
                                    start=(kc == 0), stop=(kc == qt)),
                                  reads=[f"pt{kc}", f"v{kc}"], writes=[f"ps_a{ab}"])
                            E("scalar", lambda sc, qt=qt, dh=dh, ab=ab: sc.activation(
                                asc[ab][:, dh * 512:(dh + 1) * 512], ps_a[ab][:, :], AF.Copy,
                                scale=rl_cols[:, qt:qt + 1]),
                              reads=[f"ps_a{ab}", f"rl{qt}"], writes=[f"asc{ab}"])

                    def qs_ln_steps(qs):
                        qt = qg * SUBS + qs
                        ab = qs % 2
                        sink = []
                        ln_stats(asc[ab][:], [f"asc{ab}"], ab, sink=sink)
                        ln_apply(xnt[ab][:], asc[ab][:], [f"asc{ab}"], f"xnt{ab}", ab, sink=sink)
                        sink.append(lambda: E("vector", lambda v, qt=qt, ab=ab: v.tensor_add(
                            xs[qt][:], xs[qt][:], xnt[ab][:]),
                            reads=[f"xnt{ab}", f"x{qt}"], writes=[f"x{qt}"]))
                        ln_stats(xs[qt][:], [f"x{qt}"], 2 + ab, sink=sink)
                        ln_apply(xnt[ab][:], xs[qt][:], [f"x{qt}"], f"xnt{ab}", 2 + ab, sink=sink)
                        return sink

                    for qs0 in range(0, SUBS, 2):
                        qs_front(qs0)
                        if qs0 + 1 < SUBS:
                            qs_front(qs0 + 1)
                            a_steps = qs_ln_steps(qs0)
                            b_steps = qs_ln_steps(qs0 + 1)
                            for i in range(max(len(a_steps), len(b_steps))):
                                if i < len(a_steps):
                                    a_steps[i]()
                                if i < len(b_steps):
                                    b_steps[i]()
                            transpose_to_fm(xnt[0], "xnt0", qg * SUBS + qs0)
                            transpose_to_fm(xnt[1], "xnt1", qg * SUBS + qs0 + 1)
                        else:
                            for th in qs_ln_steps(qs0):
                                th()
                            transpose_to_fm(xnt[qs0 % 2], f"xnt{qs0 % 2}", qg * SUBS + qs0)
                if dbg:
                    E("gpsimd", lambda g: g.dma_start(out=dbg_h["d_x2"][:, :], in_=xs[0][:]),
                      reads=["x0"], dma="dbg6")

        prog.barrier()
        with contextlib.ExitStack() as es1:
            gq = [sbuf(f"gq{f}", [128, QW], BF16, st=es1) for f in range(32)]
            w1st = [sbuf(f"w1st{i}", [128, 1024], F32, st=es1) for i in range(2)]
            w1bf = [sbuf(f"w1bf{i}", [128, 1024], BF16, st=es1) for i in range(2)]
            w2st = [sbuf(f"w2st{i}", [128, D], F32, st=es1) for i in range(2)]
            w2bf = [sbuf(f"w2bf{i}", [128, D], BF16, st=es1) for i in range(2)]
            b2sh = sbuf("b2sh", [128, D], F32, st=es1)
            KOFF = list(range(1, 13))
            diagk = [sbuf(f"diagk{j}", [128, 128], BF16, st=es1) for j in KOFF]
            rk = [[sbuf(f"rk{i}_{jj}", [128, QW], mybir.dt.float16, st=es1)
                   for jj in range(2)] for i in range(2)]
            for j in KOFF:
                E("vector", lambda v, j=j: v.tensor_single_scalar(
                    diagk[j - 1][:], idbf[:], AP_A[j], AO.mult),
                  reads=["idbf", "cst"], writes=[f"diagk{j}"])
            E("sync", lambda s: s.dma_start(out=b2sh[:], in_=b2t_h[:, :]),
              writes=["b2t"], dma="b2t")
            E("vector", lambda v: v.tensor_single_scalar(b2sh[:], b2sh[:], float(O_SH), AO.add),
              reads=["b2t"], writes=["b2sh"])
            vsc = [sbuf(f"vsc{i}", [128, QW], F32, st=es1) for i in range(2)]
            rsc = [sbuf(f"rsc{i}", [128, QW], F32, st=es1) for i in range(2)]
            acc = [sbuf(f"acc{i}", [128, QW], F32, st=es1) for i in range(2)]
            wsc = [sbuf(f"wsc{i}", [128, QW], F32, st=es1) for i in range(2)]
            wo = [sbuf(f"wo{i}", [128, D], F32, st=es1) for i in range(2)]
            ro = [sbuf(f"ro{i}", [128, D], F32, st=es1) for i in range(2)]
            oq = ro
            outt = wo

            def casc2(dst, dstk, src, srck, b1, b2):
                E("vector", lambda v, b1=b1, b2=b2: v._custom_dve(
                    OPS["CASC2"], out=dst, in0=src, s0=float(b1), s1=float(b2)),
                  reads=[srck], writes=[dstk])

            ps_h = [psum(f"ps_hh{i}", [128, QW], st=es1) for i in range(2)]
            ps_o = [psum(f"ps_oo{i}", [128, D // 2], st=es1) for i in range(4)]
            ps_y = [psum(f"ps_yy{i}", [128, QW], st=es1) for i in range(2)]
            pending_o = []

            def drain_pending():
                for merged in pending_o:
                    for th in merged:
                        th()
                pending_o.clear()
            for q in range(NQ if stop >= 6 else 0):
                qs_, qe_ = q * QW, (q + 1) * QW
                if True:
                    def g_chain_steps(fc):
                        pb = fc % 2
                        V, R, A_, W = vsc[pb][:], rsc[pb][:], acc[pb][:], wsc[pb][:]
                        kV, kR, kA, kW = f"vsc{pb}", f"rsc{pb}", f"acc{pb}", f"wsc{pb}"
                        steps = []

                        def vcasc(b1, b2, src, srck):
                            steps.append(lambda: E("vector", lambda v: v._custom_dve(
                                OPS["CASC2"], out=R, in0=src, s0=float(b1), s1=float(b2)),
                                reads=[srck], writes=[kR]))
                        vcasc(32.0, 16.0, V, kV)
                        vcasc(8.0, 4.0, R, kR)
                        vcasc(2.0, 1.0, R, kR)
                        vcasc(0.5, 0.25, R, kR)
                        vcasc(0.125, 1 / 16, R, kR)
                        steps.append(lambda: E("vector", lambda v: v._custom_dve(
                            OPS["CASC2F"], out=V, in0=R, in1=V, s0=1 / 32, s1=1 / 64),
                            reads=[kR, kV], writes=[kV]))

                        def sc_knot(j, jj):
                            E("scalar", lambda sc: sc.activation(
                                rk[pb][jj][:], V, AF.Relu, bias=cst[:, 34 + j:35 + j]),
                              reads=[kV, "cst"], writes=[f"rk{pb}_{jj}"])
                            E("tensor", lambda te: te.matmul(
                                ps_y[pb][:, :], diagk[j - 1][:], rk[pb][jj][:],
                                start=(j == KOFF[0]), stop=(j == KOFF[-1])),
                              reads=[f"rk{pb}_{jj}", f"diagk{j}"], writes=[f"ps_y{pb}"])
                        sc_pending = list(KOFF)

                        def with_sc(th):
                            def run():
                                th()
                                if sc_pending:
                                    j = sc_pending.pop(0)
                                    sc_knot(j, j % 2)
                            return run
                        steps.append(with_sc(lambda: E("vector", lambda v: v._custom_dve(
                            OPS["PINITHI"], out=A_, in0=V, in1=AP_A[32],
                            s0=AP_C0, s1=float(C_HI32), imm2=40.0),
                            reads=[kV, "cst"], writes=[kA])))
                        steps.append(with_sc(lambda: E("vector", lambda v: v._custom_dve(
                            OPS["KNOTLO"], out=A_, in0=V, in1=A_,
                            s0=AP_A[0], s1=float(C_LO32), imm2=8.0),
                            reads=[kV, "cst", kA], writes=[kA])))
                        for p in range(6, 15):
                            steps.append(with_sc(lambda p=p: E("vector", lambda v: v._custom_dve(
                                OPS["KNOT2"], out=A_, in0=V, in1=A_,
                                s0=AP_A[1 + 2 * p], s1=AP_A[2 + 2 * p],
                                imm2=float(9 + 2 * p)),
                                reads=[kV, "cst", kA], writes=[kA])))
                        steps.append(with_sc(lambda: E("vector", lambda v: v._custom_dve(
                            OPS["KNOT1"], out=W, in0=V, in1=A_,
                            s0=AP_A[31], s1=39.0),
                            reads=[kV, "cst", kA], writes=[kW])))
                        steps.append(lambda: E("vector", lambda v: v.tensor_add(
                            W, W, ps_y[pb][:, :]),
                            reads=[kW, f"ps_y{pb}"], writes=[kW]))
                        vcasc(4.0, 2.0, W, kW)
                        vcasc(1.0, 0.5, R, kR)
                        vcasc(0.25, 0.125, R, kR)
                        vcasc(1 / 16, 1 / 32, R, kR)
                        vcasc(1 / 64, 1 / 128, R, kR)
                        steps.append(lambda: E("vector", lambda v: v._custom_dve(
                            OPS["CASC1G"], out=gq[fc][:], in0=R, in1=W,
                            s0=1 / 256, s1=0.25),
                            reads=[kR, kW], writes=[f"g{fc}"]))
                        return steps

                    def g_front(fc):
                        pb = fc % 2
                        for kc in range(8):
                            E("sync", lambda s, fc=fc, kc=kc, pb=pb: s.dma_start(
                                out=w1st[pb][:, kc * 128:(kc + 1) * 128],
                                in_=w1_h[kc * 128:(kc + 1) * 128, fc * 128:(fc + 1) * 128]),
                              writes=[f"w1st{pb}"], dma=f"w1{pb}")
                        E("scalar", lambda sc, pb=pb: sc.activation(
                            w1bf[pb][:], w1st[pb][:], AF.Copy),
                          reads=[f"w1st{pb}"], writes=[f"w1bf{pb}"])
                        for kc in range(8):
                            E("tensor", lambda te, kc=kc, pb=pb, qs_=qs_, qe_=qe_: te.matmul(
                                ps_h[pb][:, :], w1bf[pb][:, kc * 128:(kc + 1) * 128],
                                fm[kc][:, qs_:qe_], start=(kc == 0), stop=(kc == 7)),
                              reads=[f"w1bf{pb}"] + [f"fm{kc}_{q * QTT + i}" for i in range(QTT)],
                              writes=[f"ps_h{pb}"])
                        E("scalar", lambda sc, fc=fc, pb=pb: sc.activation(
                            vsc[pb][:], ps_h[pb][:, :], AF.Relu, scale=4.0,
                            bias=biasv[:, fc:fc + 1]),
                          reads=[f"ps_h{pb}", "biasv"], writes=[f"vsc{pb}"])

                    def o_mm_fc(half, fc):
                        tts = [t_ for t_ in (2 * half, 2 * half + 1) if t_ < QTT]
                        pb = fc % 2
                        E("gpsimd", lambda g, fc=fc, pb=pb: g.dma_start(
                            out=w2st[pb][:], in_=w2_h[fc * 128:(fc + 1) * 128, :]),
                          writes=[f"w2st{pb}"], dma=f"w2{pb}_{half}")
                        E("scalar", lambda sc, pb=pb: sc.activation(
                            w2bf[pb][:], w2st[pb][:], AF.Copy),
                          reads=[f"w2st{pb}"], writes=[f"w2bf{pb}"])
                        for i, tt in enumerate(tts):
                            for dh in range(2):
                                E("tensor", lambda te, fc=fc, tt=tt, pb=pb, dh=dh, i=i: te.matmul(
                                    ps_o[2 * i + dh][:, :],
                                    gq[fc][:, tt * 128:(tt + 1) * 128],
                                    w2bf[pb][:, dh * 512:(dh + 1) * 512],
                                    start=(fc == 0), stop=(fc == 31)),
                                  reads=[f"g{fc}", f"w2bf{pb}"], writes=[f"ps_o{2 * i + dh}"])

                    for fce in range(0, 32, 2):
                        g_front(fce)
                        g_front(fce + 1)
                        if fce == 2:
                            drain_pending()
                        for s0_, s1_ in zip(g_chain_steps(fce), g_chain_steps(fce + 1)):
                            s0_()
                            s1_()
                        if fce >= 4:
                            o_mm_fc(0, fce - 4)
                            o_mm_fc(0, fce - 3)
                    if dbg and q == 0:
                        E("gpsimd", lambda g: g.dma_start(out=dbg_h["d_g0"][:, :], in_=gq[0][:]),
                          reads=["g0"], dma="dbg7")
                    def o_chain(i, tt):
                        t = q * QTT + tt
                        ob = i % 2
                        kWO, kRO = f"wo{ob}", f"ro{ob}"
                        kOQ, kOT = kRO, kWO
                        sink = []
                        E2 = _sink_E(sink)
                        for dh in range(2):
                            E2("vector", lambda v, i=i, ob=ob, dh=dh: v.tensor_add(
                                wo[ob][:, dh * 512:(dh + 1) * 512],
                                ps_o[2 * i + dh][:, :],
                                b2sh[:, dh * 512:(dh + 1) * 512]),
                               reads=[f"ps_o{2 * i + dh}", "b2sh"], writes=[kWO])
                        for b1, b2 in ((4.0, 2.0), (1.0, 0.5), (0.25, 0.125),
                                       (1 / 16, 1 / 32), (1 / 64, 1 / 128)):
                            E2("vector", lambda v, b1=b1, b2=b2, ob=ob, first=(b1 == 4.0): v._custom_dve(
                                OPS["CASC2"], out=ro[ob][:],
                                in0=(wo[ob][:] if first else ro[ob][:]),
                                s0=float(b1), s1=float(b2)),
                               reads=[kWO if b1 == 4.0 else kRO], writes=[kRO])
                        E2("vector", lambda v, ob=ob: v._custom_dve(
                            OPS["CASC1G"], out=oq[ob][:], in0=ro[ob][:], in1=wo[ob][:],
                            s0=1 / 256, s1=4.0),
                           reads=[kRO, kWO], writes=[kOQ])
                        if dbg and t == 0:
                            sink.append(lambda: E("gpsimd", lambda g, ob=ob: g.dma_start(
                                out=dbg_h["d_oq0"][:, :], in_=oq[ob][:]),
                                reads=[kOQ], dma="dbg8"))
                        ln_stats(oq[ob][:], [kOQ], ob, sink=sink)
                        ln_apply(oq[ob][:], oq[ob][:], [kOQ], kOQ, ob, sink=sink)
                        sink.append(lambda: E("vector", lambda v, t=t, ob=ob: v.tensor_add(
                            wo[ob][:], xs[t][:], oq[ob][:]),
                            reads=[f"x{t}", kOQ], writes=[kWO]))
                        sink.append(lambda: E("gpsimd", lambda g, t=t, ob=ob: g.dma_start(
                            out=out_h[t * 128:(t + 1) * 128, :], in_=wo[ob][:]),
                            reads=[kWO], writes=[f"out{t}"], dma="out"))
                        return sink

                    for fc in range(28, 32):
                        o_mm_fc(0, fc)
                    tts0 = [t_ for t_ in (0, 1) if t_ < QTT]
                    chains = [o_chain(i, tt) for i, tt in enumerate(tts0)]
                    for j in range(max(len(c) for c in chains)):
                        for c in chains:
                            if j < len(c):
                                c[j]()
                    if QTT > 2:
                        for fc in range(32):
                            o_mm_fc(1, fc)
                        chains1 = [o_chain(i, tt) for i, tt in enumerate((2, 3)) if tt < QTT]
                        merged = []
                        for j in range(max(len(c) for c in chains1)):
                            for c in chains1:
                                if j < len(c):
                                    merged.append(c[j])
                        pending_o.append(merged)
            if stop >= 6:
                drain_pending()

        prog.barrier()
        sem_names = (["s_" + e for e in ENGS]
                     + ["d_" + st for st in prog.dma_count])
        sems = {n: es.enter_context(nc.semaphore(n)) for n in sem_names}
        materialize(nc, prog, sems)
    return nc


def _host_inputs(inputs, T):
    wq = np.ascontiguousarray(inputs["L0_attn_Wq"], dtype=np.float32)
    wk = np.ascontiguousarray(inputs["L0_attn_Wk"], dtype=np.float32)
    wv = np.ascontiguousarray(inputs["L0_attn_Wv"], dtype=np.float32)
    w1 = np.ascontiguousarray(inputs["L0_ffn_w1"], dtype=np.float32)
    w2 = np.ascontiguousarray(inputs["L0_ffn_w2"], dtype=np.float32)
    b1 = np.asarray(inputs["L0_ffn_b1"], dtype=np.float32)
    b2 = np.asarray(inputs["L0_ffn_b2"], dtype=np.float32)
    b1t = np.ascontiguousarray(b1.reshape(FF // 128, 128).T)
    b2t = np.ascontiguousarray(np.tile(b2[None, :], (128, 1)))
    chi = np.ascontiguousarray(np.tile(CONST_HI[None, :], (128, 1)))
    clo = np.ascontiguousarray(np.tile(CONST_LO[None, :], (128, 1)))
    iden = np.eye(128, dtype=np.float32)
    q = np.arange(128)[:, None]
    k = np.arange(128)[None, :]
    import ml_dtypes as _md
    m1 = np.where(k <= q, 0.0, -1e30).astype(_md.bfloat16)
    madd = np.zeros((128, 4 * 512), dtype=_md.bfloat16)
    maddt = np.zeros((128, 4 * 512), dtype=_md.bfloat16)
    for v in range(4):
        madd[:, v * 512 + v * 128: v * 512 + (v + 1) * 128] = m1
        maddt[:, v * 512 + v * 128: v * 512 + (v + 1) * 128] = m1.T
    import ml_dtypes
    onesr = np.ones((1, T), dtype=ml_dtypes.bfloat16)
    base = dict(wq=wq, wk=wk, wv=wv, w1=w1, w2=w2, b1t=b1t, b2t=b2t,
                chi=chi, clo=clo, iden=iden, idbf=iden.astype(_md.bfloat16),
                madd=madd, maddt=maddt, onesr=onesr)
    return base


def kernel(L0_attn_X, L0_attn_Wq, L0_attn_Wk, L0_attn_Wv,
           L0_ffn_w1, L0_ffn_b1, L0_ffn_w2, L0_ffn_b2):
    T = int(L0_attn_X.shape[1])
    inputs = dict(L0_attn_X=L0_attn_X, L0_attn_Wq=L0_attn_Wq,
                  L0_attn_Wk=L0_attn_Wk, L0_attn_Wv=L0_attn_Wv,
                  L0_ffn_w1=L0_ffn_w1, L0_ffn_b1=L0_ffn_b1,
                  L0_ffn_w2=L0_ffn_w2, L0_ffn_b2=L0_ffn_b2)
    base = _host_inputs(inputs, T)
    X = np.asarray(L0_attn_X, dtype=np.float32)
    nc = build_nc(T=T)
    lower_extended_insts(nc)
    in_maps = []
    for b in range(NCORE):
        m = dict(base)
        m["x"] = np.ascontiguousarray(X[b])
        in_maps.append(m)
    res = run_bass_kernel_spmd(nc, in_maps, core_ids=list(range(NCORE)))
    out = np.stack([np.asarray(r["out"], dtype=np.float32) for r in res.results])
    return out
